# version 32
# baseline (speedup 1.0000x reference)
import numpy as np
import ml_dtypes
from contextlib import ExitStack

import concourse.bass as bass
import concourse.mybir as mybir
from concourse.bass_utils import run_bass_kernel_spmd

F32 = mybir.dt.float32
BF16 = mybir.dt.bfloat16
FP8 = mybir.dt.float8e4
AF = mybir.ActivationFunctionType
ALU = mybir.AluOpType
AX = mybir.AxisListType

NCORES = 8
TOK = 1024
H = 2048
I = 4096
TT = TOK // 128
TH = TOK // 512
HB = H // 128
IB = I // 128
INV224 = float(np.float32(1.0) / np.float32(224.0))


class Plan:
    ENGINES = ("sync", "gpsimd", "tensor", "vector", "scalar")

    def __init__(self):
        self.ops = {e: [] for e in self.ENGINES}
        self.cnt = {}
        self.lastwait = {}

    def c(self, sem):
        return self.cnt.get(sem, 0)

    def inc(self, sem, n=1):
        self.cnt[sem] = self.c(sem) + n
        return self.cnt[sem]

    def emit(self, eng, fn):
        self.ops[eng].append(fn)

    def wait(self, eng, sem_name, val, sems):
        if val <= 0:
            return
        key = (eng, sem_name)
        if self.lastwait.get(key, -1) >= val:
            return
        self.lastwait[key] = val
        sem = sems[sem_name]
        self.emit(eng, lambda e, s=sem, v=val: e.wait_ge(s, v))


def build_core_program():
    nc = bass.Bass("TRN2", target_bir_lowering=False)

    x0_d = nc.dram_tensor("x0", [TOK, H], F32, kind="ExternalInput").ap()
    w1_d = nc.dram_tensor("w1t", [H, 2 * I], F32, kind="ExternalInput").ap()
    w2_d = nc.dram_tensor("w2t", [I, I], F32, kind="ExternalInput").ap()
    w3_d = nc.dram_tensor("w3t", [I, H], F32, kind="ExternalInput").ap()
    w4_d = nc.dram_tensor("w4t", [H, H], F32, kind="ExternalInput").ap()
    dcol_d = nc.dram_tensor("dcol", [128, IB], F32, kind="ExternalInput").ap()
    id16_d = nc.dram_tensor("id16", [128, 128], BF16, kind="ExternalInput").ap()
    id32_d = nc.dram_tensor("id32", [128, 128], F32, kind="ExternalInput").ap()
    x1s_d = nc.dram_tensor("x1s", [TOK, H], F32).ap()
    out_d = nc.dram_tensor("out", [TOK, H], F32, kind="ExternalOutput").ap()

    ctx = ExitStack()
    sb = lambda name, cols, dt: ctx.enter_context(nc.sbuf_tensor(name, [128, cols], dt))

    A = sb("arena_a", HB * TOK, BF16)
    B = sb("arena_b", 2 * HB * TOK, BF16)
    C = sb("arena_c", IB * TOK, BF16)
    W1 = [sb("w1a", HB * 128, BF16), sb("w1b", HB * 128, BF16)]
    W2 = [sb("w2a", IB * 128, BF16), sb("w2b", IB * 128, BF16)]
    S = [sb("s_a", H, F32), sb("s_b", H, F32)]
    q8 = sb("q8", H, FP8)
    q8w = sb("q8w", 512, FP8)
    id16 = sb("id16s", 128, BF16)
    id32 = sb("id32s", 128, F32)
    dcols = sb("dcols", IB, F32)
    ones = sb("ones", 128, F32)
    eps = sb("eps", 1, F32)
    bns8a = sb("bns8a", 4 * 24, F32)
    mv8a = sb("mv8a", TT * 2, F32)
    std8a = sb("std8a", TT, F32)
    rstd8a = sb("rstd8a", TT, F32)
    mr8a = sb("mr8a", TT, F32)
    mv8b = sb("mv8b", TT * 2, F32)
    std8b = sb("std8b", TT, F32)
    rstd8b = sb("rstd8b", TT, F32)
    am16P = [sb("am160", 16, F32), sb("am161", 16, F32)]
    qsAP = [sb("qsA0", 16, F32), sb("qsA1", 16, F32)]
    dqAP = [sb("dqA0", 16, F32), sb("dqA1", 16, F32)]
    wS = sb("wS", 64, F32)
    wam = sb("wam", 1, F32)
    wqs_c = sb("wqs_c", 1, F32)
    wdq_c = sb("wdq_c", 1, F32)
    wrow = sb("wrow", 128, F32)
    WQSLOT = {2: 0, 3: 1, 0: 2, 1: 0}
    wqs_b = sb("wqs_b", 3 * 64, F32)
    wdq_b = sb("wdq_b", 3 * 64, F32)
    acc1 = sb("acc1", TT, F32)
    ssq1 = sb("ssq1", TT, F32)
    junk2 = sb("junk2", 8, F32)
    accH = sb("accH", TT * 4, F32)

    ln0T = A
    attn_tm = A
    xT = B
    zs = C
    ffn_tm = C
    W4slab = [A[:, 0:8192], A[:, 8192:16384],
              None,
              C[:, 24576:32768]]

    W4halves = {0: (W4slab[0],), 1: (W4slab[1],),
                2: (W2[0].ap() if hasattr(W2[0], "ap") else W2[0], W2[1]),
                3: (W4slab[3],)}
    W4halves[2] = (W2[0], W2[1])
    ps = [ctx.enter_context(nc.psum_tensor(f"ps{i}", [128, 512], F32)) for i in range(5)]
    psT = [ctx.enter_context(nc.psum_tensor("psTa", [128, 1024], BF16)),
           ctx.enter_context(nc.psum_tensor("psTb", [128, 1024], BF16))]
    psF = ctx.enter_context(nc.psum_tensor("psF", [128, 512], F32))

    semnames = ["qA1", "qA2", "qB1", "qB2", "qSa", "qSb", "qSa2", "qSb2", "qOa", "qOb",
                "qOc", "qOd", "qM",
                "sT", "sV", "sS", "sG",
                "qW40", "qW41", "qW42", "qW43", "qW3p", "qW3q"]
    sems = {n: ctx.enter_context(nc.semaphore(n)) for n in semnames}
    qsbuf = ["qSa", "qSb"]
    qsbuf2 = ["qSa2", "qSb2"]

    P = Plan()

    def op(eng, name, *args, inc=None, incn=1, **kw):
        if inc is not None:
            v = P.inc(inc, incn)
            sem = sems[inc]
            P.emit(eng, lambda e, a=args, k=kw, s=sem, n=incn: getattr(e, name)(*a, **k).then_inc(s, n))
            return v
        P.emit(eng, lambda e, a=args, k=kw: getattr(e, name)(*a, **k))
        return None

    def drain_inc(eng, sem_name, n=1):
        v = P.inc(sem_name, n)
        sem = sems[sem_name]
        P.emit(eng, lambda e, s=sem, n_=n: e.drain().then_inc(s, n_))
        return v

    def drain(eng):
        P.emit(eng, lambda e: e.drain())

    bank_state = [None] * 5
    bank_rr = [0]

    def acquire_bank(eng):
        b = bank_rr[0] % 5
        bank_rr[0] += 1
        if bank_state[b] is not None:
            sname, val = bank_state[b]
            P.wait(eng, sname, val, sems)
        return b

    psT_state = [None] * 2
    psT_grp = [0]

    def transpose_tile_group(src_stage, dst_tensor, tt, evac_eng="scalar"):
        v = None
        for g in range(4):
            bank = psT_grp[0] % 2
            psT_grp[0] += 1
            if psT_state[bank] is not None:
                sname, val = psT_state[bank]
                P.wait("tensor", sname, val, sems)
            for j in range(4):
                hb = g * 4 + j
                tr = op("tensor", "transpose", psT[bank][:, j * 128:(j + 1) * 128],
                        src_stage[:, hb * 128:(hb + 1) * 128], id16[:, :],
                        inc=("sT" if j == 3 else None))
            P.wait(evac_eng, "sT", tr, sems)
            sem = "sV" if evac_eng == "vector" else "sS"
            out3 = bass.AP(dst_tensor, (g * 4) * TOK + tt * 128,
                           [[dst_tensor.shape[1], 128], [TOK, 4], [1, 128]])
            v = op(evac_eng, "tensor_copy" if evac_eng == "vector" else "copy",
                   out3, psT[bank][:, 0:512].rearrange("p (b j) -> p b j", j=128), inc=sem)
            psT_state[bank] = (sem, v)
        return v, tr

    def colbcast(t, ncol_blocks, offset=0, blkstride=1):
        pstride = t.shape[1]
        return bass.AP(t, offset, [[pstride, 128], [blkstride, ncol_blocks], [0, 128]])

    op("sync", "dma_start", id16[:, :], id16_d[:, :], inc="qM", incn=16)
    op("sync", "dma_start", id32[:, :], id32_d[:, :], inc="qM", incn=16)
    op("sync", "dma_start", dcols[:, :], dcol_d[:, :], inc="qM", incn=16)
    qm_total = P.c("qM")
    op("gpsimd", "memset", ones[:, :], 1.0)
    op("gpsimd", "memset", eps[:, :], 1e-5)
    sg_ready = drain_inc("gpsimd", "sG")

    def stats_to(setname, tt, src_ap_fn):
        assert setname == "a"
        bns, mv, std = bns8a, mv8a, std8a
        sl = (tt % 4) * 24
        for i in range(4):
            op("vector", "bn_stats", bns[:, sl + i * 6: sl + (i + 1) * 6],
               src_ap_fn(i * 512, (i + 1) * 512))
        drain("vector")
        m = op("vector", "bn_aggr", mv[:, tt * 2:(tt + 1) * 2],
               bns[:, sl:sl + 24], inc="sV")
        P.wait("scalar", "sV", m, sems)
        P.wait("scalar", "sG", sg_ready, sems)
        sq = op("scalar", "activation", std[:, tt:tt + 1], mv[:, tt * 2 + 1: tt * 2 + 2],
                AF.Sqrt, inc="sS", bias=eps[:, 0:1])
        return m, sq

    def finish_rstd(setname, tt, sq, with_mr=False):
        mv, rstd = (mv8a, rstd8a) if setname == "a" else (mv8b, rstd8b)
        P.wait("vector", "sS", sq, sems)
        op("vector", "reciprocal", rstd[:, tt:tt + 1],
           (std8a if setname == "a" else std8b)[:, tt:tt + 1])
        drain("vector")
        if with_mr:
            return op("vector", "tensor_scalar", mr8a[:, tt:tt + 1], mv[:, tt * 2:tt * 2 + 1],
                      rstd[:, tt:tt + 1], -1.0, ALU.mult, ALU.mult, inc="sV")
        return drain_inc("vector", "sV")

    s_release = [0, 0]
    s_loads = [0, 0]
    ln0_evac_v = 0
    ln0_evac_s = 0

    s_loads2 = [0, 0]

    def xload(tt, src_d):
        p = tt % 2
        P.wait("sync", "sV", s_release[p], sems)
        s_loads[p] += 1
        op("sync", "dma_start", S[p][:, 0:1024], src_d[tt * 128:(tt + 1) * 128, 0:1024],
           inc=qsbuf[p], incn=16)
        P.wait("gpsimd", "sV", s_release[p], sems)
        s_loads2[p] += 1
        op("gpsimd", "dma_start", S[p][:, 1024:2048], src_d[tt * 128:(tt + 1) * 128, 1024:2048],
           inc=qsbuf2[p], incn=16)
        return 16 * s_loads[p], 16 * s_loads2[p]

    b_sq = {}
    b_need = {}
    b_need[0] = xload(0, x0_d)

    def b_stats(tt):
        p = tt % 2
        n1, n2 = b_need[tt]
        bns, mv, std = bns8a, mv8a, std8a
        sl = (tt % 4) * 24
        P.wait("vector", qsbuf[p], n1, sems)
        for i in range(2):
            op("vector", "bn_stats", bns[:, sl + i * 6: sl + (i + 1) * 6],
               S[p][:, i * 512:(i + 1) * 512])
        P.wait("vector", qsbuf2[p], n2, sems)
        for i in range(2, 4):
            op("vector", "bn_stats", bns[:, sl + i * 6: sl + (i + 1) * 6],
               S[p][:, i * 512:(i + 1) * 512])
        drain("vector")
        m = op("vector", "bn_aggr", mv[:, tt * 2:(tt + 1) * 2],
               bns[:, sl:sl + 24], inc="sV")
        P.wait("scalar", "sV", m, sems)
        P.wait("scalar", "sG", sg_ready, sems)
        b_sq[tt] = op("scalar", "activation", std[:, tt:tt + 1], mv[:, tt * 2 + 1: tt * 2 + 2],
                      AF.Sqrt, inc="sS", bias=eps[:, 0:1])

    def b_apply(tt):
        p = tt % 2
        finish_rstd("a", tt, b_sq[tt])
        stg = B[:, p * H:(p + 1) * H]
        ap_done = op("vector", "tensor_scalar", stg, S[p][:, :],
                     mv8a[:, tt * 2:tt * 2 + 1], rstd8a[:, tt:tt + 1],
                     ALU.subtract, ALU.mult, inc="sV")
        s_release[p] = ap_done
        P.wait("tensor", "qM", qm_total, sems)
        P.wait("tensor", "sV", ap_done, sems)
        v, _ = transpose_tile_group(stg, ln0T, tt, evac_eng="scalar")
        return {"scalar": v}

    b_stats(0)
    for tt in range(TT):
        if tt + 1 < TT:
            b_need[tt + 1] = xload(tt + 1, x0_d)
            b_stats(tt + 1)
        vals = b_apply(tt)
        ln0_evac_v = vals.get("vector", ln0_evac_v)
        ln0_evac_s = vals.get("scalar", ln0_evac_s)

    w_consumed = {}
    half_count = {n: 0 for n in ("qA1", "qA2", "qB1", "qB2")}
    last_on_parity = [None, None]

    def weight_dma2(stream, idx, buf, src_cols_ap, nblk, blkcols):
        pbuf = idx % 2
        key = last_on_parity[pbuf]
        if key is not None and key in w_consumed:
            P.wait("gpsimd", "sT", w_consumed[key], sems)
        last_on_parity[pbuf] = (stream, idx)
        h = nblk // 2
        sem1 = ("qA1", "qB1")[pbuf]
        sem2 = ("qA2", "qB2")[pbuf]
        half_count[sem1] += 1
        op("gpsimd", "dma_start",
           buf[:, 0:h * blkcols].rearrange("p (b c) -> p b c", c=blkcols),
           src_cols_ap[0:h * 128, :].rearrange("(b p) c -> p b c", p=128),
           inc=sem1, incn=16)
        half_count[sem2] += 1
        op("gpsimd", "dma_start",
           buf[:, h * blkcols:].rearrange("p (b c) -> p b c", c=blkcols),
           src_cols_ap[h * 128:, :].rearrange("(b p) c -> p b c", p=128),
           inc=sem2, incn=16)
        return sem1, 16 * half_count[sem1], sem2, 16 * half_count[sem2]

    P.wait("tensor", "sV", ln0_evac_v, sems)
    P.wait("tensor", "sS", ln0_evac_s, sems)
    xT_all = 0
    zs_all = 0
    for og in range(2 * IB):
        pbuf = og % 2
        s1, v1, s2, v2 = weight_dma2("w1", og, W1[pbuf],
                                     w1_d[:, og * 128:(og + 1) * 128], HB, 128)
        P.wait("tensor", s1, v1, sems)
        for th in range(TH):
            b = acquire_bank("tensor")
            for hb in range(HB):
                if hb == HB // 2:
                    P.wait("tensor", s2, v2, sems)
                last = hb == HB - 1
                g = op("tensor", "matmul", ps[b][:, :],
                       W1[pbuf][:, hb * 128:(hb + 1) * 128],
                       ln0T[:, hb * TOK + th * 512: hb * TOK + (th + 1) * 512],
                       start=(hb == 0), stop=last, inc=("sT" if last else None))
            if og < IB:
                P.wait("vector", "sT", g, sems)
                v = op("vector", "tensor_copy",
                       xT[:, og * TOK + th * 512: og * TOK + (th + 1) * 512],
                       ps[b][:, :], inc="sV")
                bank_state[b] = ("sV", v)
                xT_all = v
            else:
                P.wait("scalar", "sT", g, sems)
                v = op("scalar", "activation",
                       zs[:, (og - IB) * TOK + th * 512: (og - IB) * TOK + (th + 1) * 512],
                       ps[b][:, :], AF.Silu, inc="sS")
                bank_state[b] = ("sS", v)
                zs_all = v
        w_consumed[("w1", og)] = g

    P.wait("tensor", "sV", xT_all, sems)
    mm2_last = None
    yz_all = None
    for og in range(IB):
        pbuf = og % 2
        s1, v1, s2, v2 = weight_dma2("w2", og, W2[pbuf],
                                     w2_d[:, og * 128:(og + 1) * 128], IB, 128)
        P.wait("tensor", s1, v1, sems)
        for th in range(TH):
            b = acquire_bank("tensor")
            for kb in range(IB):
                if kb == IB // 2:
                    P.wait("tensor", s2, v2, sems)
                last = kb == IB - 1
                g = op("tensor", "matmul", ps[b][:, :],
                       W2[pbuf][:, kb * 128:(kb + 1) * 128],
                       xT[:, kb * TOK + th * 512: kb * TOK + (th + 1) * 512],
                       start=(kb == 0), stop=last, inc=("sT" if last else None))
            mm2_last = g
            P.wait("vector", "sT", g, sems)
            P.wait("vector", "sS", zs_all, sems)
            P.wait("vector", "qM", qm_total, sems)
            cols = slice(og * TOK + th * 512, og * TOK + (th + 1) * 512)
            v = op("vector", "scalar_tensor_tensor", zs[:, cols], ps[b][:, :],
                   dcols[:, og:og + 1], zs[:, cols], ALU.mult, ALU.mult, inc="sV")
            bank_state[b] = ("sV", v)
            yz_all = v
        w_consumed[("w2", og)] = mm2_last

    w4sem = ["qW40", "qW41", "qW42", "qW43"]
    f1_mr = {}
    x1done = {}
    ln2done = {}
    ln2q_free = [0]
    ln2stageP = [B[:, HB * TOK: HB * TOK + H],
                 B[:, HB * TOK + 3 * H: HB * TOK + 4 * H]]
    ln2qstage = B[:, HB * TOK + H: HB * TOK + 2 * H]
    junk = B[:, HB * TOK + 2 * H: HB * TOK + 3 * H]
    qo_cnt = [0, 0]
    qosem = ["qOa", "qOb"]
    x1_stores_p = [0, 0]
    q2_evac_s = 0
    wq_done = {}
    wq_scales_done = {}
    wq_prev_reads = [0]
    q8_free = [0]
    f2s_done = {}

    def w4_dma(og, gate, gate2=None):
        halves = W4halves[og]
        if len(halves) == 1:
            P.wait("gpsimd", gate[0], gate[1], sems)
            op("gpsimd", "dma_start",
               halves[0].rearrange("p (b c) -> p b c", c=512),
               w4_d[:, og * 512:(og + 1) * 512].rearrange("(b p) c -> p b c", p=128),
               inc=w4sem[og], incn=16)
            return 16
        P.wait("gpsimd", gate[0], gate[1], sems)
        op("gpsimd", "dma_start",
           halves[0].rearrange("p (b c) -> p b c", c=512),
           w4_d[0:1024, og * 512:(og + 1) * 512].rearrange("(b p) c -> p b c", p=128),
           inc=w4sem[og], incn=16)
        if gate2 is not None:
            P.wait("gpsimd", gate2[0], gate2[1], sems)
        op("gpsimd", "dma_start",
           halves[1].rearrange("p (b c) -> p b c", c=512),
           w4_d[1024:2048, og * 512:(og + 1) * 512].rearrange("(b p) c -> p b c", p=128),
           inc=w4sem[og], incn=16)
        return 32


    def wq_scales(og, dma_need=16):
        P.wait("vector", w4sem[og], dma_need, sems)
        P.wait("vector", "sT", wq_prev_reads[0], sems)
        halves = W4halves[og]
        if len(halves) == 1:
            op("vector", "tensor_reduce", wS[:, :],
               halves[0].rearrange("p (b c) -> p b c", c=128), AX.X, ALU.max,
               apply_absolute_value=True)
        else:
            op("vector", "tensor_reduce", wS[:, 0:32],
               halves[0].rearrange("p (b c) -> p b c", c=128), AX.X, ALU.max,
               apply_absolute_value=True)
            op("vector", "tensor_reduce", wS[:, 32:64],
               halves[1].rearrange("p (b c) -> p b c", c=128), AX.X, ALU.max,
               apply_absolute_value=True)
        v1 = drain_inc("vector", "sV")
        P.wait("tensor", "sV", v1, sems)
        tr1 = op("tensor", "transpose", psF[0:64, 0:128], wS[:, 0:64], id32[:, :], inc="sT")
        P.wait("vector", "sT", tr1, sems)
        op("vector", "tensor_reduce", wam[0:64, 0:1], psF[0:64, 0:128], AX.X, ALU.max)
        drain("vector")
        op("vector", "tensor_scalar_max", wam[0:64, 0:1], wam[0:64, 0:1], 1e-4)
        drain("vector")
        op("vector", "reciprocal", wqs_c[0:64, 0:1], wam[0:64, 0:1])
        drain("vector")
        op("vector", "tensor_scalar", wqs_c[0:64, 0:1], wqs_c[0:64, 0:1], 224.0, None, ALU.mult)
        op("vector", "tensor_scalar", wdq_c[0:64, 0:1], wam[0:64, 0:1], INV224, None, ALU.mult)
        drain("vector")
        op("vector", "tensor_scalar", wrow[0:64, 0:128], ones[0:64, 0:128],
           wqs_c[0:64, 0:1], None, ALU.mult)
        v2 = drain_inc("vector", "sV")
        P.wait("tensor", "sV", v2, sems)
        tr2 = op("tensor", "transpose", psF[:, 128:192], wrow[0:64, 0:128],
                 id32[0:64, 0:64], inc="sT")
        P.wait("vector", "sT", tr2, sems)
        op("vector", "tensor_copy", wqs_b[:, WQSLOT[og] * 64:(WQSLOT[og] + 1) * 64], psF[:, 128:192])
        drain("vector")
        op("vector", "tensor_scalar", wrow[0:64, 0:128], ones[0:64, 0:128],
           wdq_c[0:64, 0:1], None, ALU.mult)
        v3 = drain_inc("vector", "sV")
        P.wait("tensor", "sV", v3, sems)
        tr3 = op("tensor", "transpose", psF[:, 192:256], wrow[0:64, 0:128],
                 id32[0:64, 0:64], inc="sT")
        wq_prev_reads[0] = tr3
        P.wait("vector", "sT", tr3, sems)
        op("vector", "tensor_copy", wdq_b[:, WQSLOT[og] * 64:(WQSLOT[og] + 1) * 64], psF[:, 192:256])
        wq_scales_done[og] = drain_inc("vector", "sV")

    def wq_rt_chunk(og, ch):
        halves = W4halves[og]
        if len(halves) == 1:
            part = halves[0][:, ch * 2048:(ch + 1) * 2048]
        else:
            part = halves[ch // 2][:, (ch % 2) * 2048:((ch % 2) + 1) * 2048]
        op("vector", "tensor_tensor",
           q8.rearrange("p (b j) -> p b j", j=128),
           part.rearrange("p (b j) -> p b j", j=128),
           colbcast(wqs_b, 16, offset=WQSLOT[og] * 64 + ch * 16), ALU.mult)
        drain("vector")
        d = op("vector", "tensor_tensor",
               part.rearrange("p (b j) -> p b j", j=128),
               q8.rearrange("p (b j) -> p b j", j=128),
               colbcast(wdq_b, 16, offset=WQSLOT[og] * 64 + ch * 16), ALU.mult,
               inc=("sV" if ch == 3 else None))
        drain("vector")
        if ch == 3:
            wq_done[og] = [("sV", d)]
            q8_free[0] = d

    def _wq_part(og, ch):
        halves = W4halves[og]
        if len(halves) == 1:
            return halves[0][:, ch * 2048:(ch + 1) * 2048]
        return halves[ch // 2][:, (ch % 2) * 2048:((ch % 2) + 1) * 2048]

    def wq_rt_v_sub(og, ch, sub, mark=False):
        pp = _wq_part(og, ch)[:, sub * 512:(sub + 1) * 512]
        off = WQSLOT[og] * 64 + ch * 16 + sub * 4
        P.wait("vector", "sV", wq_scales_done[og], sems)
        op("vector", "tensor_tensor",
           q8w.rearrange("p (b j) -> p b j", j=128),
           pp.rearrange("p (b j) -> p b j", j=128),
           colbcast(wqs_b, 4, offset=off), ALU.mult)
        drain("vector")
        d = op("vector", "tensor_tensor",
               pp.rearrange("p (b j) -> p b j", j=128),
               q8w.rearrange("p (b j) -> p b j", j=128),
               colbcast(wdq_b, 4, offset=off), ALU.mult,
               inc=("sV" if mark else None))
        drain("vector")
        return d

    def wq_rt_vc(og, ch, mark=False):
        d = None
        for sub in range(4):
            d = wq_rt_v_sub(og, ch, sub, mark=(mark and sub == 3))
        return d

    sq2 = {}
    subd_m = {}
    def f_load(tt):
        p = tt % 2
        P.wait("sync", "sV", s_release[p], sems)
        P.wait("sync", qosem[p], 16 * x1_stores_p[p], sems)
        s_loads[p] += 1
        op("sync", "dma_start", S[p][:, :], x0_d[tt * 128:(tt + 1) * 128, :],
           inc=qsbuf[p], incn=16)

    def f_subd(tt):
        p = tt % 2
        P.wait("scalar", "sV", f1_mr[tt], sems)
        P.wait("scalar", qsbuf[p], 16 * s_loads[p], sems)
        P.wait("scalar", qsbuf2[p], 16 * s_loads2[p], sems)
        subd_m[tt] = op("scalar", "add", S[p][:, :], S[p][:, :], mr8a[:, tt:tt + 1],
                        inc="sS")

    def f_stt(tt):
        p = tt % 2
        P.wait("vector", "sS", subd_m[tt], sems)
        x1done[tt] = op("vector", "scalar_tensor_tensor", S[p][:, :],
                        attn_tm[:, tt * H:(tt + 1) * H], rstd8a[:, tt:tt + 1], S[p][:, :],
                        ALU.mult, ALU.add, accum_out=acc1[:, tt:tt + 1], inc="sV")
        drain("vector")

    def f_st1b(tt):
        p = tt % 2
        P.wait("scalar", "sV", x1done[tt], sems)
        x1_stores_p[p] += 1
        op("scalar", "dma_start", x1s_d[tt * 128:(tt + 1) * 128, :], S[p][:, :],
           inc=qosem[p], incn=16)
        op("scalar", "activation", junk[:, :], S[p][:, :], AF.Square,
           accum_out=ssq1[:, tt:tt + 1])
        op("scalar", "mul", mv8b[:, tt * 2:tt * 2 + 1], acc1[:, tt:tt + 1], 1.0 / H)
        drain("scalar")
        op("scalar", "mul", junk2[:, 0:1], mv8b[:, tt * 2:tt * 2 + 1],
           mv8b[:, tt * 2:tt * 2 + 1])
        op("scalar", "mul", junk2[:, 2:3], ssq1[:, tt:tt + 1], 1.0 / H)
        drain("scalar")
        op("scalar", "mul", junk2[:, 1:2], junk2[:, 0:1], -1.0)
        drain("scalar")
        op("scalar", "add", mv8b[:, tt * 2 + 1:tt * 2 + 2], junk2[:, 2:3], junk2[:, 1:2])
        drain("scalar")
        sq2[tt] = op("scalar", "activation", std8b[:, tt:tt + 1],
                     mv8b[:, tt * 2 + 1:tt * 2 + 2], AF.Sqrt, inc="sS", bias=eps[:, 0:1])

    def f_st2a(tt):
        p = tt % 2
        stg = ln2stageP[p]
        P.wait("vector", "sS", sq2[tt], sems)
        op("vector", "reciprocal", rstd8b[:, tt:tt + 1], std8b[:, tt:tt + 1])
        drain("vector")
        lnd = op("vector", "tensor_scalar", stg[:, :], S[p][:, :],
                 mv8b[:, tt * 2:tt * 2 + 1], rstd8b[:, tt:tt + 1],
                 ALU.subtract, ALU.mult, inc="sV")
        drain("vector")
        ln2done[tt] = lnd
        s_release[p] = lnd
        op("vector", "tensor_reduce", am16P[p][:, :],
           stg.rearrange("p (b c) -> p b c", c=128), AX.X, ALU.max,
           apply_absolute_value=True)
        drain("vector")
        op("vector", "tensor_scalar_max", am16P[p][:, :], am16P[p][:, :], 1e-4)
        drain("vector")
        op("vector", "reciprocal", qsAP[p][:, :], am16P[p][:, :])
        drain("vector")
        op("vector", "tensor_scalar", qsAP[p][:, :], qsAP[p][:, :], 224.0, None, ALU.mult)
        op("vector", "tensor_scalar", dqAP[p][:, :], am16P[p][:, :], INV224, None, ALU.mult)
        drain("vector")

    q_done = {}

    def f_q(tt):
        p = tt % 2
        op("vector", "tensor_tensor",
           q8.rearrange("p (b j) -> p b j", j=128),
           ln2stageP[p].rearrange("p (b j) -> p b j", j=128),
           colbcast(qsAP[p], HB), ALU.mult)
        drain("vector")
        P.wait("vector", "sT", ln2q_free[0], sems)
        d = op("vector", "tensor_tensor",
               ln2qstage.rearrange("p (b j) -> p b j", j=128),
               q8.rearrange("p (b j) -> p b j", j=128),
               colbcast(dqAP[p], HB), ALU.mult, inc="sV")
        drain("vector")
        q_done[tt] = d

    def f_tr(tt):
        nonlocal q2_evac_s
        P.wait("tensor", "sV", q_done[tt], sems)
        v, tr = transpose_tile_group(ln2qstage, B, tt, evac_eng="scalar")
        ln2q_free[0] = tr
        q2_evac_s = v

    def mm4_group(og, tt):
        halves = W4halves[og]
        b = acquire_bank("tensor")
        for kb in range(HB):
            if len(halves) == 1:
                rhs = halves[0][:, kb * 512:(kb + 1) * 512]
            else:
                rhs = halves[kb // 8][:, (kb % 8) * 512:((kb % 8) + 1) * 512]
            last = kb == HB - 1
            g = op("tensor", "matmul", ps[b][:, :],
                   B[:, kb * TOK + tt * 128: kb * TOK + (tt + 1) * 128],
                   rhs,
                   start=(kb == 0), stop=last, inc=("sT" if last else None))
        P.wait("scalar", "sT", g, sems)
        v = op("scalar", "activation",
               ffn_tm[:, tt * H + og * 512: tt * H + (og + 1) * 512],
               ps[b][:, :], AF.Copy, accum_out=accH[:, tt * 4 + og: tt * 4 + og + 1],
               inc="sS")
        bank_state[b] = ("sS", v)
        return v

    P.wait("tensor", "sV", yz_all, sems)
    f_load(0)
    f_load(1)
    P.wait("gpsimd", "sT", w_consumed[("w1", 2 * IB - 1)], sems)
    op("gpsimd", "dma_start", W1[0][:, :].rearrange("p (b c) -> p b c", c=512),
       w3_d[0:512, 0:512].rearrange("(b p) c -> p b c", p=128), inc="qW3p", incn=16)
    op("gpsimd", "dma_start", W1[1][:, :].rearrange("p (b c) -> p b c", c=512),
       w3_d[512:1024, 0:512].rearrange("(b p) c -> p b c", p=128), inc="qW3q", incn=16)
    attn_done = {}
    mm3_last = None
    f1_sq = {}
    for hs in range(4):
        pbuf = hs % 2
        slab = B[:, pbuf * (HB * TOK):(pbuf * (HB * TOK)) + IB * 512]
        if hs < 2:
            P.wait("gpsimd", "sT", mm2_last, sems)
        if hs == 0:
            qsems = ["qA2", "qB1", "qB2"]
            qneed = [None]
            for qi, q in enumerate((1, 2, 3)):
                half_count[qsems[qi]] += 1
                op("gpsimd", "dma_start",
                   slab[:, q * 4096:(q + 1) * 4096].rearrange("p (b c) -> p b c", c=512),
                   w3_d[q * 1024:(q + 1) * 1024, 0:512].rearrange("(b p) c -> p b c", p=128),
                   inc=qsems[qi], incn=16)
                qneed.append((qsems[qi], 16 * half_count[qsems[qi]]))
            last_on_parity[0] = ("w3q", 0)
            last_on_parity[1] = ("w3q", 0)
        else:
            s1, v1, s2, v2 = weight_dma2("w3", hs, slab,
                                         w3_d[:, hs * 512:(hs + 1) * 512], IB, 512)
        if hs == 1:
            w4_dma(2, ("sT", w_consumed[("w2", IB - 2)]),
                   gate2=("sT", w_consumed[("w2", IB - 1)]))
        if hs == 0:
            P.wait("tensor", "qW3p", 16, sems)
            P.wait("tensor", "qW3q", 16, sems)
        else:
            P.wait("tensor", s1, v1, sems)
        for tt in range(TT):
            b = acquire_bank("tensor")
            for kb in range(IB):
                if hs == 0:
                    if kb in (8, 16, 24):
                        P.wait("tensor", qneed[kb // 8][0], qneed[kb // 8][1], sems)
                elif kb == IB // 2:
                    P.wait("tensor", s2, v2, sems)
                last = kb == IB - 1
                if hs == 0 and kb < 8:
                    rhs = W1[kb // 4][:, (kb % 4) * 512:((kb % 4) + 1) * 512]
                else:
                    rhs = slab[:, kb * 512:(kb + 1) * 512]
                g = op("tensor", "matmul", ps[b][:, :],
                       zs[:, kb * TOK + tt * 128: kb * TOK + (tt + 1) * 128],
                       rhs,
                       start=(kb == 0), stop=last, inc=("sT" if last else None))
            P.wait("scalar", "sT", g, sems)
            v = op("scalar", "copy",
                   attn_tm[:, tt * H + hs * 512: tt * H + (hs + 1) * 512],
                   ps[b][:, :], inc="sS")
            bank_state[b] = ("sS", v)
            if hs == 0 and tt == 0:
                w_consumed[("w3q", 0)] = g
            if hs == 3:
                attn_done[tt] = v
                P.wait("vector", "sS", v, sems)
                _, sq = stats_to("a", tt, lambda a, b_, tt=tt: attn_tm[:, tt * H + a: tt * H + b_])
                f1_sq[tt] = sq
                f1_mr[tt] = finish_rstd("a", tt, sq, with_mr=True)
                if tt == 1:
                    f_subd(0)
                    f_stt(0)
                if tt == 2:
                    f_subd(1)
                    f_stt(1)
        w_consumed[("w3", hs)] = g
        mm3_last = g
        if hs == 0:
            last_on_parity[0] = ("w3", 0)
        if hs == 2:
            wq_scales(2, dma_need=32)
            for ch in range(4):
                wq_rt_chunk(2, ch)




    w4_dma(3, ("sT", mm3_last))
    wq3_last = None
    RT3 = {3: (0, 3), 4: (3, 6), 5: (6, 9), 6: (9, 12), 7: (12, 16)}
    for t in range(TT):
        if 1 <= t and t + 1 < TT:
            f_stt(t + 1)
        f_st1b(t)
        if t >= 1:
            f_tr(t - 1)
        f_st2a(t)
        f_q(t)
        if t >= 1:
            if t == 1:
                for s_, v_ in wq_done[2]:
                    P.wait("tensor", s_, v_, sems)
            P.wait("tensor", "sS", q2_evac_s, sems)
            mm4_group(2, t - 1)
        if t + 2 < TT:
            f_load(t + 2)
            f_subd(t + 2)
        if t == 2:
            wq_scales(3)
        if t == 3:
            w4_dma(0, ("sV", x1done[3]))
        if t >= 3:
            for gidx in range(*RT3[t]):
                wq3_last = wq_rt_v_sub(3, gidx // 4, gidx % 4, mark=(gidx == 15))
    f_tr(TT - 1)
    P.wait("tensor", "sS", q2_evac_s, sems)
    mm4_group(2, TT - 1)
    wq_done[3] = [("sV", wq3_last)]
    w4_dma(1, ("sV", x1done[7]))

    out_stores_p = [0, 0]
    out2_p = [0, 0]
    qosem2 = ["qOc", "qOd"]
    h_srel = [0, 0]
    h_sq = {}
    h_loads = {}

    def h_load(tt):
        p = tt % 2
        rel = s_release[p] if tt < 2 else h_srel[p]
        P.wait("sync", qosem[p], 16 * (x1_stores_p[p] + out_stores_p[p]), sems)
        P.wait("sync", qosem2[p], 16 * out2_p[p], sems)
        P.wait("sync", "sV", rel, sems)
        s_loads[p] += 1
        op("sync", "dma_start", S[p][:, 0:1024], x1s_d[tt * 128:(tt + 1) * 128, 0:1024],
           inc=qsbuf[p], incn=16)
        P.wait("gpsimd", qosem[p], 16 * (x1_stores_p[p] + out_stores_p[p]), sems)
        P.wait("gpsimd", qosem2[p], 16 * out2_p[p], sems)
        P.wait("gpsimd", "sV", rel, sems)
        s_loads2[p] += 1
        op("gpsimd", "dma_start", S[p][:, 1024:2048], x1s_d[tt * 128:(tt + 1) * 128, 1024:2048],
           inc=qsbuf2[p], incn=16)
        h_loads[tt] = (16 * s_loads[p], 16 * s_loads2[p])

    def h_stats(tt):
        drain("scalar")
        op("scalar", "activation", junk[:, :], ffn_tm[:, tt * H:(tt + 1) * H],
           AF.Square, accum_out=ssq1[:, tt:tt + 1])
        drain("scalar")
        op("scalar", "add", junk2[:, 3:4], accH[:, tt * 4:tt * 4 + 1],
           accH[:, tt * 4 + 1:tt * 4 + 2])
        op("scalar", "add", junk2[:, 4:5], accH[:, tt * 4 + 2:tt * 4 + 3],
           accH[:, tt * 4 + 3:tt * 4 + 4])
        op("scalar", "mul", junk2[:, 2:3], ssq1[:, tt:tt + 1], 1.0 / H)
        drain("scalar")
        op("scalar", "add", junk2[:, 3:4], junk2[:, 3:4], junk2[:, 4:5])
        drain("scalar")
        op("scalar", "mul", mv8a[:, tt * 2:tt * 2 + 1], junk2[:, 3:4], 1.0 / H)
        drain("scalar")
        op("scalar", "mul", junk2[:, 0:1], mv8a[:, tt * 2:tt * 2 + 1],
           mv8a[:, tt * 2:tt * 2 + 1])
        drain("scalar")
        op("scalar", "mul", junk2[:, 1:2], junk2[:, 0:1], -1.0)
        drain("scalar")
        op("scalar", "add", mv8a[:, tt * 2 + 1:tt * 2 + 2], junk2[:, 2:3], junk2[:, 1:2])
        drain("scalar")
        h_sq[tt] = op("scalar", "activation", std8a[:, tt:tt + 1],
                      mv8a[:, tt * 2 + 1:tt * 2 + 2], AF.Sqrt, inc="sS", bias=eps[:, 0:1])

    def h_st2(tt):
        p = tt % 2
        P.wait("vector", "sS", h_sq[tt], sems)
        op("vector", "reciprocal", rstd8a[:, tt:tt + 1], std8a[:, tt:tt + 1])
        drain("vector")
        op("vector", "tensor_scalar", mr8a[:, tt:tt + 1], mv8a[:, tt * 2:tt * 2 + 1],
           rstd8a[:, tt:tt + 1], None, ALU.mult)
        drain("vector")
        P.wait("vector", qsbuf[p], h_loads[tt][0], sems)
        P.wait("vector", qsbuf2[p], h_loads[tt][1], sems)
        op("vector", "tensor_scalar", S[p][:, :], S[p][:, :], mr8a[:, tt:tt + 1],
           None, ALU.subtract)
        drain("vector")
        odone = op("vector", "scalar_tensor_tensor", S[p][:, :],
                   ffn_tm[:, tt * H:(tt + 1) * H], rstd8a[:, tt:tt + 1], S[p][:, :],
                   ALU.mult, ALU.add, inc="sV")
        drain("vector")
        h_srel[p] = odone
        P.wait("scalar", "sV", odone, sems)
        out_stores_p[p] += 1
        op("scalar", "dma_start", out_d[tt * 128:(tt + 1) * 128, 0:1024], S[p][:, 0:1024],
           inc=qosem[p], incn=16)
        P.wait("sync", "sV", odone, sems)
        out2_p[p] += 1
        op("sync", "dma_start", out_d[tt * 128:(tt + 1) * 128, 1024:2048], S[p][:, 1024:2048],
           inc=qosem2[p], incn=16)

    h_load(0)
    h_load(1)
    for s_, v_ in wq_done[3]:
        P.wait("tensor", s_, v_, sems)
    for tt in range(TT):
        mm4_group(3, tt)
        if tt == 1:
            wq_scales(1)
    d1 = None
    for ch in range(4):
        d1 = wq_rt_vc(1, ch, mark=(ch == 3))
    wq_done[1] = [("sV", d1)]
    for s_, v_ in wq_done[1]:
        P.wait("tensor", s_, v_, sems)
    for tt in range(TT):
        mm4_group(1, tt)
        if tt == 1:
            wq_scales(0)
    d0 = None
    for ch in range(4):
        d0 = wq_rt_vc(0, ch, mark=(ch == 3))
    wq_done[0] = [("sV", d0)]
    for s_, v_ in wq_done[0]:
        P.wait("tensor", s_, v_, sems)
    mm4_group(0, 0)
    h_stats(0)
    for tt in range(TT):
        if tt + 1 < TT:
            mm4_group(0, tt + 1)
            h_stats(tt + 1)
        h_st2(tt)
        if tt + 2 < TT:
            h_load(tt + 2)

    P.wait("sync", "qOa", 16 * (x1_stores_p[0] + out_stores_p[0]), sems)
    P.wait("sync", "qOb", 16 * (x1_stores_p[1] + out_stores_p[1]), sems)
    P.wait("sync", "qOc", 16 * out2_p[0], sems)
    P.wait("sync", "qOd", 16 * out2_p[1], sems)

    with nc.Block() as blk:
        @blk.sync
        def _(e):
            for f in P.ops["sync"]:
                f(e)

        @blk.gpsimd
        def _(e):
            for f in P.ops["gpsimd"]:
                f(e)

        @blk.tensor
        def _(e):
            for f in P.ops["tensor"]:
                f(e)

        @blk.vector
        def _(e):
            for f in P.ops["vector"]:
                f(e)

        @blk.scalar
        def _(e):
            for f in P.ops["scalar"]:
                f(e)

    ctx.close()
    return nc


def kernel(L0_attn_hidden_states, L0_attn_W_in, L0_attn_conv_w, L0_attn_W_x,
           L0_attn_W_dt, L0_attn_b_dt, L0_attn_D, L0_attn_W_out, L0_ffn_weight):
    x = np.ascontiguousarray(np.asarray(L0_attn_hidden_states, dtype=np.float32).reshape(NCORES * TOK, H))
    w1t = np.ascontiguousarray(np.asarray(L0_attn_W_in, dtype=np.float32).T)
    w2t = np.ascontiguousarray(np.asarray(L0_attn_conv_w, dtype=np.float32).T)
    w3t = np.ascontiguousarray(np.asarray(L0_attn_W_out, dtype=np.float32).T)
    w4t = np.ascontiguousarray(np.asarray(L0_ffn_weight, dtype=np.float32).T)
    dcol = np.ascontiguousarray(np.asarray(L0_attn_D, dtype=np.float32).reshape(IB, 128).T)
    id16 = np.eye(128, dtype=ml_dtypes.bfloat16)
    id32 = np.eye(128, dtype=np.float32)

    nc = build_core_program()
    in_maps = []
    for c in range(NCORES):
        in_maps.append({
            "x0": x[c * TOK:(c + 1) * TOK],
            "w1t": w1t, "w2t": w2t, "w3t": w3t, "w4t": w4t,
            "dcol": dcol, "id16": id16, "id32": id32,
        })
    res = run_bass_kernel_spmd(nc, in_maps, core_ids=list(range(NCORES)))
    out = np.concatenate([np.asarray(r["out"]) for r in res.results], axis=0)
    return out.reshape(4, 2048, H).astype(np.float32)


# version 33
# speedup vs baseline: 1.0250x; 1.0250x over previous
import numpy as np
import ml_dtypes
from contextlib import ExitStack

import concourse.bass as bass
import concourse.mybir as mybir
from concourse.bass_utils import run_bass_kernel_spmd

F32 = mybir.dt.float32
BF16 = mybir.dt.bfloat16
FP8 = mybir.dt.float8e4
AF = mybir.ActivationFunctionType
ALU = mybir.AluOpType
AX = mybir.AxisListType

NCORES = 8
TOK = 1024
H = 2048
I = 4096
TT = TOK // 128
TH = TOK // 512
HB = H // 128
IB = I // 128
INV224 = float(np.float32(1.0) / np.float32(224.0))


class Plan:
    ENGINES = ("sync", "gpsimd", "tensor", "vector", "scalar")

    def __init__(self):
        self.ops = {e: [] for e in self.ENGINES}
        self.cnt = {}
        self.lastwait = {}

    def c(self, sem):
        return self.cnt.get(sem, 0)

    def inc(self, sem, n=1):
        self.cnt[sem] = self.c(sem) + n
        return self.cnt[sem]

    def emit(self, eng, fn):
        self.ops[eng].append(fn)

    def wait(self, eng, sem_name, val, sems):
        if val <= 0:
            return
        key = (eng, sem_name)
        if self.lastwait.get(key, -1) >= val:
            return
        self.lastwait[key] = val
        sem = sems[sem_name]
        self.emit(eng, lambda e, s=sem, v=val: e.wait_ge(s, v))


def build_core_program():
    nc = bass.Bass("TRN2", target_bir_lowering=False)

    x0_d = nc.dram_tensor("x0", [TOK, H], F32, kind="ExternalInput").ap()
    w1_d = nc.dram_tensor("w1t", [H, 2 * I], F32, kind="ExternalInput").ap()
    w2_d = nc.dram_tensor("w2t", [I, I], F32, kind="ExternalInput").ap()
    w3_d = nc.dram_tensor("w3t", [I, H], F32, kind="ExternalInput").ap()
    w4_d = nc.dram_tensor("w4t", [H, H], F32, kind="ExternalInput").ap()
    dcol_d = nc.dram_tensor("dcol", [128, IB], F32, kind="ExternalInput").ap()
    id16_d = nc.dram_tensor("id16", [128, 128], BF16, kind="ExternalInput").ap()
    id32_d = nc.dram_tensor("id32", [128, 128], F32, kind="ExternalInput").ap()
    x1s_d = nc.dram_tensor("x1s", [TOK, H], F32).ap()
    out_d = nc.dram_tensor("out", [TOK, H], F32, kind="ExternalOutput").ap()

    ctx = ExitStack()
    sb = lambda name, cols, dt: ctx.enter_context(nc.sbuf_tensor(name, [128, cols], dt))

    A = sb("arena_a", HB * TOK, BF16)
    B = sb("arena_b", 2 * HB * TOK, BF16)
    C = sb("arena_c", IB * TOK, BF16)
    W1 = [sb("w1a", HB * 128, BF16), sb("w1b", HB * 128, BF16)]
    W2 = [sb("w2a", IB * 128, BF16), sb("w2b", IB * 128, BF16)]
    S = [sb("s_a", H, F32), sb("s_b", H, F32)]
    q8 = sb("q8", H, FP8)
    q8w = sb("q8w", 512, FP8)
    id16 = sb("id16s", 128, BF16)
    id32 = sb("id32s", 128, F32)
    dcols = sb("dcols", IB, F32)
    ones = sb("ones", 128, F32)
    eps = sb("eps", 1, F32)
    bns8a = sb("bns8a", 4 * 24, F32)
    mv8a = sb("mv8a", TT * 2, F32)
    std8a = sb("std8a", TT, F32)
    rstd8a = sb("rstd8a", TT, F32)
    mr8a = sb("mr8a", TT, F32)
    mv8b = sb("mv8b", TT * 2, F32)
    std8b = sb("std8b", TT, F32)
    rstd8b = sb("rstd8b", TT, F32)
    am16P = [sb("am160", 16, F32), sb("am161", 16, F32)]
    qsAP = [sb("qsA0", 16, F32), sb("qsA1", 16, F32)]
    dqAP = [sb("dqA0", 16, F32), sb("dqA1", 16, F32)]
    wS = sb("wS", 64, F32)
    wam = sb("wam", 1, F32)
    wqs_c = sb("wqs_c", 1, F32)
    wdq_c = sb("wdq_c", 1, F32)
    wrow = sb("wrow", 128, F32)
    WQSLOT = {2: 0, 3: 1, 0: 2, 1: 0}
    wqs_b = sb("wqs_b", 3 * 64, F32)
    wdq_b = sb("wdq_b", 3 * 64, F32)
    acc1 = sb("acc1", TT, F32)
    ssq1 = sb("ssq1", TT, F32)
    junk2 = sb("junk2", 8, F32)
    accH = sb("accH", TT * 4, F32)

    ln0T = A
    attn_tm = A
    xT = B
    zs = C
    ffn_tm = C
    W4slab = [A[:, 0:8192], A[:, 8192:16384],
              None,
              C[:, 24576:32768]]

    W4halves = {0: (W4slab[0],), 1: (W4slab[1],),
                2: (W2[0].ap() if hasattr(W2[0], "ap") else W2[0], W2[1]),
                3: (W4slab[3],)}
    W4halves[2] = (W2[0], W2[1])
    ps = [ctx.enter_context(nc.psum_tensor(f"ps{i}", [128, 512], F32)) for i in range(5)]
    psT = [ctx.enter_context(nc.psum_tensor("psTa", [128, 1024], BF16)),
           ctx.enter_context(nc.psum_tensor("psTb", [128, 1024], BF16))]
    psF = ctx.enter_context(nc.psum_tensor("psF", [128, 512], F32))

    semnames = ["qA1", "qA2", "qB1", "qB2", "qSa", "qSb", "qSa2", "qSb2", "qOa", "qOb",
                "qOc", "qOd", "qM",
                "sT", "sV", "sS", "sG",
                "qW40", "qW41", "qW42", "qW43", "qW3p", "qW3q"]
    sems = {n: ctx.enter_context(nc.semaphore(n)) for n in semnames}
    qsbuf = ["qSa", "qSb"]
    qsbuf2 = ["qSa2", "qSb2"]

    P = Plan()

    def op(eng, name, *args, inc=None, incn=1, **kw):
        if inc is not None:
            v = P.inc(inc, incn)
            sem = sems[inc]
            P.emit(eng, lambda e, a=args, k=kw, s=sem, n=incn: getattr(e, name)(*a, **k).then_inc(s, n))
            return v
        P.emit(eng, lambda e, a=args, k=kw: getattr(e, name)(*a, **k))
        return None

    def drain_inc(eng, sem_name, n=1):
        v = P.inc(sem_name, n)
        sem = sems[sem_name]
        P.emit(eng, lambda e, s=sem, n_=n: e.drain().then_inc(s, n_))
        return v

    def drain(eng):
        P.emit(eng, lambda e: e.drain())

    bank_state = [None] * 5
    bank_rr = [0]

    def acquire_bank(eng):
        b = bank_rr[0] % 5
        bank_rr[0] += 1
        if bank_state[b] is not None:
            sname, val = bank_state[b]
            P.wait(eng, sname, val, sems)
        return b

    psT_state = [None] * 2
    psT_grp = [0]

    def transpose_tile_group(src_stage, dst_tensor, tt, evac_eng="scalar"):
        v = None
        for g in range(4):
            bank = psT_grp[0] % 2
            psT_grp[0] += 1
            if psT_state[bank] is not None:
                sname, val = psT_state[bank]
                P.wait("tensor", sname, val, sems)
            for j in range(4):
                hb = g * 4 + j
                tr = op("tensor", "transpose", psT[bank][:, j * 128:(j + 1) * 128],
                        src_stage[:, hb * 128:(hb + 1) * 128], id16[:, :],
                        inc=("sT" if j == 3 else None))
            P.wait(evac_eng, "sT", tr, sems)
            sem = "sV" if evac_eng == "vector" else "sS"
            out3 = bass.AP(dst_tensor, (g * 4) * TOK + tt * 128,
                           [[dst_tensor.shape[1], 128], [TOK, 4], [1, 128]])
            v = op(evac_eng, "tensor_copy" if evac_eng == "vector" else "copy",
                   out3, psT[bank][:, 0:512].rearrange("p (b j) -> p b j", j=128), inc=sem)
            psT_state[bank] = (sem, v)
        return v, tr

    def colbcast(t, ncol_blocks, offset=0, blkstride=1):
        pstride = t.shape[1]
        return bass.AP(t, offset, [[pstride, 128], [blkstride, ncol_blocks], [0, 128]])

    op("gpsimd", "memset", ones[:, :], 1.0)
    op("gpsimd", "memset", eps[:, :], 1e-5)
    sg_ready = drain_inc("gpsimd", "sG")

    def stats_to(setname, tt, src_ap_fn):
        assert setname == "a"
        bns, mv, std = bns8a, mv8a, std8a
        sl = (tt % 4) * 24
        for i in range(4):
            op("vector", "bn_stats", bns[:, sl + i * 6: sl + (i + 1) * 6],
               src_ap_fn(i * 512, (i + 1) * 512))
        drain("vector")
        m = op("vector", "bn_aggr", mv[:, tt * 2:(tt + 1) * 2],
               bns[:, sl:sl + 24], inc="sV")
        P.wait("scalar", "sV", m, sems)
        P.wait("scalar", "sG", sg_ready, sems)
        sq = op("scalar", "activation", std[:, tt:tt + 1], mv[:, tt * 2 + 1: tt * 2 + 2],
                AF.Sqrt, inc="sS", bias=eps[:, 0:1])
        return m, sq

    def finish_rstd(setname, tt, sq, with_mr=False):
        mv, rstd = (mv8a, rstd8a) if setname == "a" else (mv8b, rstd8b)
        P.wait("vector", "sS", sq, sems)
        op("vector", "reciprocal", rstd[:, tt:tt + 1],
           (std8a if setname == "a" else std8b)[:, tt:tt + 1])
        drain("vector")
        if with_mr:
            return op("vector", "tensor_scalar", mr8a[:, tt:tt + 1], mv[:, tt * 2:tt * 2 + 1],
                      rstd[:, tt:tt + 1], -1.0, ALU.mult, ALU.mult, inc="sV")
        return drain_inc("vector", "sV")

    s_release = [0, 0]
    s_loads = [0, 0]
    ln0_evac_v = 0
    ln0_evac_s = 0

    s_loads2 = [0, 0]

    def xload(tt, src_d):
        p = tt % 2
        P.wait("sync", "sV", s_release[p], sems)
        s_loads[p] += 1
        op("sync", "dma_start", S[p][:, 0:1024], src_d[tt * 128:(tt + 1) * 128, 0:1024],
           inc=qsbuf[p], incn=16)
        P.wait("gpsimd", "sV", s_release[p], sems)
        s_loads2[p] += 1
        op("gpsimd", "dma_start", S[p][:, 1024:2048], src_d[tt * 128:(tt + 1) * 128, 1024:2048],
           inc=qsbuf2[p], incn=16)
        return 16 * s_loads[p], 16 * s_loads2[p]

    b_sq = {}
    b_need = {}
    b_need[0] = xload(0, x0_d)
    op("sync", "dma_start", id16[:, :], id16_d[:, :], inc="qM", incn=16)
    op("sync", "dma_start", id32[:, :], id32_d[:, :], inc="qM", incn=16)
    op("sync", "dma_start", dcols[:, :], dcol_d[:, :], inc="qM", incn=16)
    qm_total = P.c("qM")

    def b_stats(tt):
        p = tt % 2
        n1, n2 = b_need[tt]
        bns, mv, std = bns8a, mv8a, std8a
        sl = (tt % 4) * 24
        P.wait("vector", qsbuf[p], n1, sems)
        for i in range(2):
            op("vector", "bn_stats", bns[:, sl + i * 6: sl + (i + 1) * 6],
               S[p][:, i * 512:(i + 1) * 512])
        P.wait("vector", qsbuf2[p], n2, sems)
        for i in range(2, 4):
            op("vector", "bn_stats", bns[:, sl + i * 6: sl + (i + 1) * 6],
               S[p][:, i * 512:(i + 1) * 512])
        drain("vector")
        m = op("vector", "bn_aggr", mv[:, tt * 2:(tt + 1) * 2],
               bns[:, sl:sl + 24], inc="sV")
        P.wait("scalar", "sV", m, sems)
        P.wait("scalar", "sG", sg_ready, sems)
        b_sq[tt] = op("scalar", "activation", std[:, tt:tt + 1], mv[:, tt * 2 + 1: tt * 2 + 2],
                      AF.Sqrt, inc="sS", bias=eps[:, 0:1])

    def b_apply(tt):
        p = tt % 2
        finish_rstd("a", tt, b_sq[tt])
        stg = B[:, p * H:(p + 1) * H]
        ap_done = op("vector", "tensor_scalar", stg, S[p][:, :],
                     mv8a[:, tt * 2:tt * 2 + 1], rstd8a[:, tt:tt + 1],
                     ALU.subtract, ALU.mult, inc="sV")
        s_release[p] = ap_done
        P.wait("tensor", "qM", qm_total, sems)
        P.wait("tensor", "sV", ap_done, sems)
        v, _ = transpose_tile_group(stg, ln0T, tt, evac_eng="scalar")
        return {"scalar": v}

    b_stats(0)
    for tt in range(TT):
        if tt + 1 < TT:
            b_need[tt + 1] = xload(tt + 1, x0_d)
            b_stats(tt + 1)
        vals = b_apply(tt)
        ln0_evac_v = vals.get("vector", ln0_evac_v)
        ln0_evac_s = vals.get("scalar", ln0_evac_s)

    w_consumed = {}
    half_count = {n: 0 for n in ("qA1", "qA2", "qB1", "qB2")}
    last_on_parity = [None, None]

    def weight_dma2(stream, idx, buf, src_cols_ap, nblk, blkcols):
        pbuf = idx % 2
        key = last_on_parity[pbuf]
        if key is not None and key in w_consumed:
            P.wait("gpsimd", "sT", w_consumed[key], sems)
        last_on_parity[pbuf] = (stream, idx)
        h = nblk // 2
        sem1 = ("qA1", "qB1")[pbuf]
        sem2 = ("qA2", "qB2")[pbuf]
        half_count[sem1] += 1
        op("gpsimd", "dma_start",
           buf[:, 0:h * blkcols].rearrange("p (b c) -> p b c", c=blkcols),
           src_cols_ap[0:h * 128, :].rearrange("(b p) c -> p b c", p=128),
           inc=sem1, incn=16)
        half_count[sem2] += 1
        op("gpsimd", "dma_start",
           buf[:, h * blkcols:].rearrange("p (b c) -> p b c", c=blkcols),
           src_cols_ap[h * 128:, :].rearrange("(b p) c -> p b c", p=128),
           inc=sem2, incn=16)
        return sem1, 16 * half_count[sem1], sem2, 16 * half_count[sem2]

    P.wait("tensor", "sV", ln0_evac_v, sems)
    P.wait("tensor", "sS", ln0_evac_s, sems)
    xT_all = 0
    zs_all = 0
    for og in range(2 * IB):
        pbuf = og % 2
        s1, v1, s2, v2 = weight_dma2("w1", og, W1[pbuf],
                                     w1_d[:, og * 128:(og + 1) * 128], HB, 128)
        P.wait("tensor", s1, v1, sems)
        for th in range(TH):
            b = acquire_bank("tensor")
            for hb in range(HB):
                if hb == HB // 2:
                    P.wait("tensor", s2, v2, sems)
                last = hb == HB - 1
                g = op("tensor", "matmul", ps[b][:, :],
                       W1[pbuf][:, hb * 128:(hb + 1) * 128],
                       ln0T[:, hb * TOK + th * 512: hb * TOK + (th + 1) * 512],
                       start=(hb == 0), stop=last, inc=("sT" if last else None))
            if og < IB:
                P.wait("vector", "sT", g, sems)
                v = op("vector", "tensor_copy",
                       xT[:, og * TOK + th * 512: og * TOK + (th + 1) * 512],
                       ps[b][:, :], inc="sV")
                bank_state[b] = ("sV", v)
                xT_all = v
            else:
                P.wait("scalar", "sT", g, sems)
                v = op("scalar", "activation",
                       zs[:, (og - IB) * TOK + th * 512: (og - IB) * TOK + (th + 1) * 512],
                       ps[b][:, :], AF.Silu, inc="sS")
                bank_state[b] = ("sS", v)
                zs_all = v
        w_consumed[("w1", og)] = g

    P.wait("tensor", "sV", xT_all, sems)
    mm2_last = None
    yz_all = None
    for og in range(IB):
        pbuf = og % 2
        s1, v1, s2, v2 = weight_dma2("w2", og, W2[pbuf],
                                     w2_d[:, og * 128:(og + 1) * 128], IB, 128)
        P.wait("tensor", s1, v1, sems)
        for th in range(TH):
            b = acquire_bank("tensor")
            for kb in range(IB):
                if kb == IB // 2:
                    P.wait("tensor", s2, v2, sems)
                last = kb == IB - 1
                g = op("tensor", "matmul", ps[b][:, :],
                       W2[pbuf][:, kb * 128:(kb + 1) * 128],
                       xT[:, kb * TOK + th * 512: kb * TOK + (th + 1) * 512],
                       start=(kb == 0), stop=last, inc=("sT" if last else None))
            mm2_last = g
            P.wait("vector", "sT", g, sems)
            P.wait("vector", "sS", zs_all, sems)
            P.wait("vector", "qM", qm_total, sems)
            cols = slice(og * TOK + th * 512, og * TOK + (th + 1) * 512)
            v = op("vector", "scalar_tensor_tensor", zs[:, cols], ps[b][:, :],
                   dcols[:, og:og + 1], zs[:, cols], ALU.mult, ALU.mult, inc="sV")
            bank_state[b] = ("sV", v)
            yz_all = v
        w_consumed[("w2", og)] = mm2_last

    w4sem = ["qW40", "qW41", "qW42", "qW43"]
    f1_mr = {}
    x1done = {}
    ln2done = {}
    ln2q_free = [0]
    ln2stageP = [B[:, HB * TOK: HB * TOK + H],
                 B[:, HB * TOK + 3 * H: HB * TOK + 4 * H]]
    ln2qstage = B[:, HB * TOK + H: HB * TOK + 2 * H]
    junk = B[:, HB * TOK + 2 * H: HB * TOK + 3 * H]
    qo_cnt = [0, 0]
    qosem = ["qOa", "qOb"]
    x1_stores_p = [0, 0]
    q2_evac_s = 0
    wq_done = {}
    wq_scales_done = {}
    wq_prev_reads = [0]
    q8_free = [0]
    f2s_done = {}

    def w4_dma(og, gate, gate2=None):
        halves = W4halves[og]
        if len(halves) == 1:
            P.wait("gpsimd", gate[0], gate[1], sems)
            op("gpsimd", "dma_start",
               halves[0].rearrange("p (b c) -> p b c", c=512),
               w4_d[:, og * 512:(og + 1) * 512].rearrange("(b p) c -> p b c", p=128),
               inc=w4sem[og], incn=16)
            return 16
        P.wait("gpsimd", gate[0], gate[1], sems)
        op("gpsimd", "dma_start",
           halves[0].rearrange("p (b c) -> p b c", c=512),
           w4_d[0:1024, og * 512:(og + 1) * 512].rearrange("(b p) c -> p b c", p=128),
           inc=w4sem[og], incn=16)
        if gate2 is not None:
            P.wait("gpsimd", gate2[0], gate2[1], sems)
        op("gpsimd", "dma_start",
           halves[1].rearrange("p (b c) -> p b c", c=512),
           w4_d[1024:2048, og * 512:(og + 1) * 512].rearrange("(b p) c -> p b c", p=128),
           inc=w4sem[og], incn=16)
        return 32


    def wq_scales(og, dma_need=16):
        P.wait("vector", w4sem[og], dma_need, sems)
        P.wait("vector", "sT", wq_prev_reads[0], sems)
        halves = W4halves[og]
        if len(halves) == 1:
            op("vector", "tensor_reduce", wS[:, :],
               halves[0].rearrange("p (b c) -> p b c", c=128), AX.X, ALU.max,
               apply_absolute_value=True)
        else:
            op("vector", "tensor_reduce", wS[:, 0:32],
               halves[0].rearrange("p (b c) -> p b c", c=128), AX.X, ALU.max,
               apply_absolute_value=True)
            op("vector", "tensor_reduce", wS[:, 32:64],
               halves[1].rearrange("p (b c) -> p b c", c=128), AX.X, ALU.max,
               apply_absolute_value=True)
        v1 = drain_inc("vector", "sV")
        P.wait("tensor", "sV", v1, sems)
        tr1 = op("tensor", "transpose", psF[0:64, 0:128], wS[:, 0:64], id32[:, :], inc="sT")
        P.wait("vector", "sT", tr1, sems)
        op("vector", "tensor_reduce", wam[0:64, 0:1], psF[0:64, 0:128], AX.X, ALU.max)
        drain("vector")
        op("vector", "tensor_scalar_max", wam[0:64, 0:1], wam[0:64, 0:1], 1e-4)
        drain("vector")
        op("vector", "reciprocal", wqs_c[0:64, 0:1], wam[0:64, 0:1])
        drain("vector")
        op("vector", "tensor_scalar", wqs_c[0:64, 0:1], wqs_c[0:64, 0:1], 224.0, None, ALU.mult)
        op("vector", "tensor_scalar", wdq_c[0:64, 0:1], wam[0:64, 0:1], INV224, None, ALU.mult)
        drain("vector")
        op("vector", "tensor_scalar", wrow[0:64, 0:128], ones[0:64, 0:128],
           wqs_c[0:64, 0:1], None, ALU.mult)
        v2 = drain_inc("vector", "sV")
        P.wait("tensor", "sV", v2, sems)
        tr2 = op("tensor", "transpose", psF[:, 128:192], wrow[0:64, 0:128],
                 id32[0:64, 0:64], inc="sT")
        P.wait("vector", "sT", tr2, sems)
        op("vector", "tensor_copy", wqs_b[:, WQSLOT[og] * 64:(WQSLOT[og] + 1) * 64], psF[:, 128:192])
        drain("vector")
        op("vector", "tensor_scalar", wrow[0:64, 0:128], ones[0:64, 0:128],
           wdq_c[0:64, 0:1], None, ALU.mult)
        v3 = drain_inc("vector", "sV")
        P.wait("tensor", "sV", v3, sems)
        tr3 = op("tensor", "transpose", psF[:, 192:256], wrow[0:64, 0:128],
                 id32[0:64, 0:64], inc="sT")
        wq_prev_reads[0] = tr3
        P.wait("vector", "sT", tr3, sems)
        op("vector", "tensor_copy", wdq_b[:, WQSLOT[og] * 64:(WQSLOT[og] + 1) * 64], psF[:, 192:256])
        wq_scales_done[og] = drain_inc("vector", "sV")

    def wq_rt_chunk(og, ch):
        halves = W4halves[og]
        if len(halves) == 1:
            part = halves[0][:, ch * 2048:(ch + 1) * 2048]
        else:
            part = halves[ch // 2][:, (ch % 2) * 2048:((ch % 2) + 1) * 2048]
        op("vector", "tensor_tensor",
           q8.rearrange("p (b j) -> p b j", j=128),
           part.rearrange("p (b j) -> p b j", j=128),
           colbcast(wqs_b, 16, offset=WQSLOT[og] * 64 + ch * 16), ALU.mult)
        drain("vector")
        d = op("vector", "tensor_tensor",
               part.rearrange("p (b j) -> p b j", j=128),
               q8.rearrange("p (b j) -> p b j", j=128),
               colbcast(wdq_b, 16, offset=WQSLOT[og] * 64 + ch * 16), ALU.mult,
               inc=("sV" if ch == 3 else None))
        drain("vector")
        if ch == 3:
            wq_done[og] = [("sV", d)]
            q8_free[0] = d

    def _wq_part(og, ch):
        halves = W4halves[og]
        if len(halves) == 1:
            return halves[0][:, ch * 2048:(ch + 1) * 2048]
        return halves[ch // 2][:, (ch % 2) * 2048:((ch % 2) + 1) * 2048]

    def wq_rt_v_sub(og, ch, sub, mark=False):
        pp = _wq_part(og, ch)[:, sub * 512:(sub + 1) * 512]
        off = WQSLOT[og] * 64 + ch * 16 + sub * 4
        P.wait("vector", "sV", wq_scales_done[og], sems)
        op("vector", "tensor_tensor",
           q8w.rearrange("p (b j) -> p b j", j=128),
           pp.rearrange("p (b j) -> p b j", j=128),
           colbcast(wqs_b, 4, offset=off), ALU.mult)
        drain("vector")
        d = op("vector", "tensor_tensor",
               pp.rearrange("p (b j) -> p b j", j=128),
               q8w.rearrange("p (b j) -> p b j", j=128),
               colbcast(wdq_b, 4, offset=off), ALU.mult,
               inc=("sV" if mark else None))
        drain("vector")
        return d

    def wq_rt_vc(og, ch, mark=False):
        d = None
        for sub in range(4):
            d = wq_rt_v_sub(og, ch, sub, mark=(mark and sub == 3))
        return d

    sq2 = {}
    subd_m = {}
    def f_load(tt):
        p = tt % 2
        P.wait("sync", "sV", s_release[p], sems)
        P.wait("sync", qosem[p], 16 * x1_stores_p[p], sems)
        s_loads[p] += 1
        op("sync", "dma_start", S[p][:, :], x0_d[tt * 128:(tt + 1) * 128, :],
           inc=qsbuf[p], incn=16)

    def f_subd(tt):
        p = tt % 2
        P.wait("scalar", "sV", f1_mr[tt], sems)
        P.wait("scalar", qsbuf[p], 16 * s_loads[p], sems)
        P.wait("scalar", qsbuf2[p], 16 * s_loads2[p], sems)
        subd_m[tt] = op("scalar", "add", S[p][:, :], S[p][:, :], mr8a[:, tt:tt + 1],
                        inc="sS")

    def f_stt(tt):
        p = tt % 2
        P.wait("vector", "sS", subd_m[tt], sems)
        x1done[tt] = op("vector", "scalar_tensor_tensor", S[p][:, :],
                        attn_tm[:, tt * H:(tt + 1) * H], rstd8a[:, tt:tt + 1], S[p][:, :],
                        ALU.mult, ALU.add, accum_out=acc1[:, tt:tt + 1], inc="sV")
        drain("vector")

    def f_st1b(tt):
        p = tt % 2
        P.wait("scalar", "sV", x1done[tt], sems)
        x1_stores_p[p] += 1
        op("scalar", "dma_start", x1s_d[tt * 128:(tt + 1) * 128, :], S[p][:, :],
           inc=qosem[p], incn=16)
        op("scalar", "activation", junk[:, :], S[p][:, :], AF.Square,
           accum_out=ssq1[:, tt:tt + 1])
        op("scalar", "mul", mv8b[:, tt * 2:tt * 2 + 1], acc1[:, tt:tt + 1], 1.0 / H)
        drain("scalar")
        op("scalar", "mul", junk2[:, 0:1], mv8b[:, tt * 2:tt * 2 + 1],
           mv8b[:, tt * 2:tt * 2 + 1])
        op("scalar", "mul", junk2[:, 2:3], ssq1[:, tt:tt + 1], 1.0 / H)
        drain("scalar")
        op("scalar", "mul", junk2[:, 1:2], junk2[:, 0:1], -1.0)
        drain("scalar")
        op("scalar", "add", mv8b[:, tt * 2 + 1:tt * 2 + 2], junk2[:, 2:3], junk2[:, 1:2])
        drain("scalar")
        sq2[tt] = op("scalar", "activation", std8b[:, tt:tt + 1],
                     mv8b[:, tt * 2 + 1:tt * 2 + 2], AF.Sqrt, inc="sS", bias=eps[:, 0:1])

    def f_st2a(tt):
        p = tt % 2
        stg = ln2stageP[p]
        P.wait("vector", "sS", sq2[tt], sems)
        op("vector", "reciprocal", rstd8b[:, tt:tt + 1], std8b[:, tt:tt + 1])
        drain("vector")
        lnd = op("vector", "tensor_scalar", stg[:, :], S[p][:, :],
                 mv8b[:, tt * 2:tt * 2 + 1], rstd8b[:, tt:tt + 1],
                 ALU.subtract, ALU.mult, inc="sV")
        drain("vector")
        ln2done[tt] = lnd
        s_release[p] = lnd
        op("vector", "tensor_reduce", am16P[p][:, :],
           stg.rearrange("p (b c) -> p b c", c=128), AX.X, ALU.max,
           apply_absolute_value=True)
        drain("vector")
        op("vector", "tensor_scalar_max", am16P[p][:, :], am16P[p][:, :], 1e-4)
        drain("vector")
        op("vector", "reciprocal", qsAP[p][:, :], am16P[p][:, :])
        drain("vector")
        op("vector", "tensor_scalar", qsAP[p][:, :], qsAP[p][:, :], 224.0, None, ALU.mult)
        op("vector", "tensor_scalar", dqAP[p][:, :], am16P[p][:, :], INV224, None, ALU.mult)
        drain("vector")

    q_done = {}

    def f_q(tt):
        p = tt % 2
        op("vector", "tensor_tensor",
           q8.rearrange("p (b j) -> p b j", j=128),
           ln2stageP[p].rearrange("p (b j) -> p b j", j=128),
           colbcast(qsAP[p], HB), ALU.mult)
        drain("vector")
        P.wait("vector", "sT", ln2q_free[0], sems)
        d = op("vector", "tensor_tensor",
               ln2qstage.rearrange("p (b j) -> p b j", j=128),
               q8.rearrange("p (b j) -> p b j", j=128),
               colbcast(dqAP[p], HB), ALU.mult, inc="sV")
        drain("vector")
        q_done[tt] = d

    def f_tr(tt):
        nonlocal q2_evac_s
        P.wait("tensor", "sV", q_done[tt], sems)
        v, tr = transpose_tile_group(ln2qstage, B, tt, evac_eng="scalar")
        ln2q_free[0] = tr
        q2_evac_s = v

    def mm4_group(og, tt):
        halves = W4halves[og]
        b = acquire_bank("tensor")
        for kb in range(HB):
            if len(halves) == 1:
                rhs = halves[0][:, kb * 512:(kb + 1) * 512]
            else:
                rhs = halves[kb // 8][:, (kb % 8) * 512:((kb % 8) + 1) * 512]
            last = kb == HB - 1
            g = op("tensor", "matmul", ps[b][:, :],
                   B[:, kb * TOK + tt * 128: kb * TOK + (tt + 1) * 128],
                   rhs,
                   start=(kb == 0), stop=last, inc=("sT" if last else None))
        P.wait("scalar", "sT", g, sems)
        v = op("scalar", "activation",
               ffn_tm[:, tt * H + og * 512: tt * H + (og + 1) * 512],
               ps[b][:, :], AF.Copy, accum_out=accH[:, tt * 4 + og: tt * 4 + og + 1],
               inc="sS")
        bank_state[b] = ("sS", v)
        return v

    P.wait("tensor", "sV", yz_all, sems)
    f_load(0)
    f_load(1)
    P.wait("gpsimd", "sT", w_consumed[("w1", 2 * IB - 1)], sems)
    op("gpsimd", "dma_start", W1[0][:, :].rearrange("p (b c) -> p b c", c=512),
       w3_d[0:512, 0:512].rearrange("(b p) c -> p b c", p=128), inc="qW3p", incn=16)
    op("gpsimd", "dma_start", W1[1][:, :].rearrange("p (b c) -> p b c", c=512),
       w3_d[512:1024, 0:512].rearrange("(b p) c -> p b c", p=128), inc="qW3q", incn=16)
    attn_done = {}
    mm3_last = None
    f1_sq = {}
    for hs in range(4):
        pbuf = hs % 2
        slab = B[:, pbuf * (HB * TOK):(pbuf * (HB * TOK)) + IB * 512]
        if hs < 2:
            P.wait("gpsimd", "sT", mm2_last, sems)
        if hs == 0:
            qsems = ["qA2", "qB1", "qB2"]
            qneed = [None]
            for qi, q in enumerate((1, 2, 3)):
                half_count[qsems[qi]] += 1
                op("gpsimd", "dma_start",
                   slab[:, q * 4096:(q + 1) * 4096].rearrange("p (b c) -> p b c", c=512),
                   w3_d[q * 1024:(q + 1) * 1024, 0:512].rearrange("(b p) c -> p b c", p=128),
                   inc=qsems[qi], incn=16)
                qneed.append((qsems[qi], 16 * half_count[qsems[qi]]))
            last_on_parity[0] = ("w3q", 0)
            last_on_parity[1] = ("w3q", 0)
        else:
            s1, v1, s2, v2 = weight_dma2("w3", hs, slab,
                                         w3_d[:, hs * 512:(hs + 1) * 512], IB, 512)
        if hs == 1:
            w4_dma(2, ("sT", w_consumed[("w2", IB - 2)]),
                   gate2=("sT", w_consumed[("w2", IB - 1)]))
        if hs == 0:
            P.wait("tensor", "qW3p", 16, sems)
            P.wait("tensor", "qW3q", 16, sems)
        else:
            P.wait("tensor", s1, v1, sems)
        for tt in range(TT):
            b = acquire_bank("tensor")
            for kb in range(IB):
                if hs == 0:
                    if kb in (8, 16, 24):
                        P.wait("tensor", qneed[kb // 8][0], qneed[kb // 8][1], sems)
                elif kb == IB // 2:
                    P.wait("tensor", s2, v2, sems)
                last = kb == IB - 1
                if hs == 0 and kb < 8:
                    rhs = W1[kb // 4][:, (kb % 4) * 512:((kb % 4) + 1) * 512]
                else:
                    rhs = slab[:, kb * 512:(kb + 1) * 512]
                g = op("tensor", "matmul", ps[b][:, :],
                       zs[:, kb * TOK + tt * 128: kb * TOK + (tt + 1) * 128],
                       rhs,
                       start=(kb == 0), stop=last, inc=("sT" if last else None))
            P.wait("scalar", "sT", g, sems)
            v = op("scalar", "copy",
                   attn_tm[:, tt * H + hs * 512: tt * H + (hs + 1) * 512],
                   ps[b][:, :], inc="sS")
            bank_state[b] = ("sS", v)
            if hs == 0 and tt == 0:
                w_consumed[("w3q", 0)] = g
            if hs == 3:
                attn_done[tt] = v
                P.wait("vector", "sS", v, sems)
                _, sq = stats_to("a", tt, lambda a, b_, tt=tt: attn_tm[:, tt * H + a: tt * H + b_])
                f1_sq[tt] = sq
                f1_mr[tt] = finish_rstd("a", tt, sq, with_mr=True)
                if tt == 4:
                    f_subd(0)
                    f_stt(0)
                if tt == 5:
                    f_subd(1)
                    f_stt(1)
        w_consumed[("w3", hs)] = g
        mm3_last = g
        if hs == 0:
            last_on_parity[0] = ("w3", 0)
        if hs == 2:
            wq_scales(2, dma_need=32)
            for ch in range(4):
                wq_rt_chunk(2, ch)




    w4_dma(3, ("sT", mm3_last))
    wq3_last = None
    RT3 = {3: (0, 3), 4: (3, 6), 5: (6, 9), 6: (9, 12), 7: (12, 16)}
    for t in range(TT):
        if 1 <= t and t + 1 < TT:
            f_stt(t + 1)
        f_st1b(t)
        if t >= 1:
            f_tr(t - 1)
        f_st2a(t)
        f_q(t)
        if t >= 1:
            if t == 1:
                for s_, v_ in wq_done[2]:
                    P.wait("tensor", s_, v_, sems)
            P.wait("tensor", "sS", q2_evac_s, sems)
            mm4_group(2, t - 1)
        if t + 2 < TT:
            f_load(t + 2)
            f_subd(t + 2)
        if t == 2:
            wq_scales(3)
        if t == 3:
            w4_dma(0, ("sV", x1done[3]))
        if t >= 3:
            for gidx in range(*RT3[t]):
                wq3_last = wq_rt_v_sub(3, gidx // 4, gidx % 4, mark=(gidx == 15))
    f_tr(TT - 1)
    P.wait("tensor", "sS", q2_evac_s, sems)
    mm4_group(2, TT - 1)
    wq_done[3] = [("sV", wq3_last)]
    w4_dma(1, ("sV", x1done[7]))

    out_stores_p = [0, 0]
    out2_p = [0, 0]
    qosem2 = ["qOc", "qOd"]
    h_srel = [0, 0]
    h_sq = {}
    h_loads = {}

    def h_load(tt):
        p = tt % 2
        rel = s_release[p] if tt < 2 else h_srel[p]
        P.wait("sync", qosem[p], 16 * (x1_stores_p[p] + out_stores_p[p]), sems)
        P.wait("sync", qosem2[p], 16 * out2_p[p], sems)
        P.wait("sync", "sV", rel, sems)
        s_loads[p] += 1
        op("sync", "dma_start", S[p][:, 0:1024], x1s_d[tt * 128:(tt + 1) * 128, 0:1024],
           inc=qsbuf[p], incn=16)
        P.wait("gpsimd", qosem[p], 16 * (x1_stores_p[p] + out_stores_p[p]), sems)
        P.wait("gpsimd", qosem2[p], 16 * out2_p[p], sems)
        P.wait("gpsimd", "sV", rel, sems)
        s_loads2[p] += 1
        op("gpsimd", "dma_start", S[p][:, 1024:2048], x1s_d[tt * 128:(tt + 1) * 128, 1024:2048],
           inc=qsbuf2[p], incn=16)
        h_loads[tt] = (16 * s_loads[p], 16 * s_loads2[p])

    def h_stats(tt):
        drain("scalar")
        op("scalar", "activation", junk[:, :], ffn_tm[:, tt * H:(tt + 1) * H],
           AF.Square, accum_out=ssq1[:, tt:tt + 1])
        drain("scalar")
        op("scalar", "add", junk2[:, 3:4], accH[:, tt * 4:tt * 4 + 1],
           accH[:, tt * 4 + 1:tt * 4 + 2])
        op("scalar", "add", junk2[:, 4:5], accH[:, tt * 4 + 2:tt * 4 + 3],
           accH[:, tt * 4 + 3:tt * 4 + 4])
        op("scalar", "mul", junk2[:, 2:3], ssq1[:, tt:tt + 1], 1.0 / H)
        drain("scalar")
        op("scalar", "add", junk2[:, 3:4], junk2[:, 3:4], junk2[:, 4:5])
        drain("scalar")
        op("scalar", "mul", mv8a[:, tt * 2:tt * 2 + 1], junk2[:, 3:4], 1.0 / H)
        drain("scalar")
        op("scalar", "mul", junk2[:, 0:1], mv8a[:, tt * 2:tt * 2 + 1],
           mv8a[:, tt * 2:tt * 2 + 1])
        drain("scalar")
        op("scalar", "mul", junk2[:, 1:2], junk2[:, 0:1], -1.0)
        drain("scalar")
        op("scalar", "add", mv8a[:, tt * 2 + 1:tt * 2 + 2], junk2[:, 2:3], junk2[:, 1:2])
        drain("scalar")
        h_sq[tt] = op("scalar", "activation", std8a[:, tt:tt + 1],
                      mv8a[:, tt * 2 + 1:tt * 2 + 2], AF.Sqrt, inc="sS", bias=eps[:, 0:1])

    def h_st2(tt):
        p = tt % 2
        P.wait("vector", "sS", h_sq[tt], sems)
        op("vector", "reciprocal", rstd8a[:, tt:tt + 1], std8a[:, tt:tt + 1])
        drain("vector")
        op("vector", "tensor_scalar", mr8a[:, tt:tt + 1], mv8a[:, tt * 2:tt * 2 + 1],
           rstd8a[:, tt:tt + 1], None, ALU.mult)
        drain("vector")
        P.wait("vector", qsbuf[p], h_loads[tt][0], sems)
        P.wait("vector", qsbuf2[p], h_loads[tt][1], sems)
        op("vector", "tensor_scalar", S[p][:, :], S[p][:, :], mr8a[:, tt:tt + 1],
           None, ALU.subtract)
        drain("vector")
        odone = op("vector", "scalar_tensor_tensor", S[p][:, :],
                   ffn_tm[:, tt * H:(tt + 1) * H], rstd8a[:, tt:tt + 1], S[p][:, :],
                   ALU.mult, ALU.add, inc="sV")
        drain("vector")
        h_srel[p] = odone
        P.wait("scalar", "sV", odone, sems)
        out_stores_p[p] += 1
        op("scalar", "dma_start", out_d[tt * 128:(tt + 1) * 128, 0:1024], S[p][:, 0:1024],
           inc=qosem[p], incn=16)
        P.wait("sync", "sV", odone, sems)
        out2_p[p] += 1
        op("sync", "dma_start", out_d[tt * 128:(tt + 1) * 128, 1024:2048], S[p][:, 1024:2048],
           inc=qosem2[p], incn=16)

    h_load(0)
    h_load(1)
    for s_, v_ in wq_done[3]:
        P.wait("tensor", s_, v_, sems)
    for tt in range(TT):
        mm4_group(3, tt)
        if tt == 1:
            wq_scales(1)
    d1 = None
    for ch in range(4):
        d1 = wq_rt_vc(1, ch, mark=(ch == 3))
    wq_done[1] = [("sV", d1)]
    for s_, v_ in wq_done[1]:
        P.wait("tensor", s_, v_, sems)
    for tt in range(TT):
        mm4_group(1, tt)
        if tt == 1:
            wq_scales(0)
    d0 = None
    for ch in range(4):
        d0 = wq_rt_vc(0, ch, mark=(ch == 3))
    wq_done[0] = [("sV", d0)]
    for s_, v_ in wq_done[0]:
        P.wait("tensor", s_, v_, sems)
    mm4_group(0, 0)
    h_stats(0)
    for tt in range(TT):
        if tt + 1 < TT:
            mm4_group(0, tt + 1)
            h_stats(tt + 1)
        h_st2(tt)
        if tt + 2 < TT:
            h_load(tt + 2)

    P.wait("sync", "qOa", 16 * (x1_stores_p[0] + out_stores_p[0]), sems)
    P.wait("sync", "qOb", 16 * (x1_stores_p[1] + out_stores_p[1]), sems)
    P.wait("sync", "qOc", 16 * out2_p[0], sems)
    P.wait("sync", "qOd", 16 * out2_p[1], sems)

    with nc.Block() as blk:
        @blk.sync
        def _(e):
            for f in P.ops["sync"]:
                f(e)

        @blk.gpsimd
        def _(e):
            for f in P.ops["gpsimd"]:
                f(e)

        @blk.tensor
        def _(e):
            for f in P.ops["tensor"]:
                f(e)

        @blk.vector
        def _(e):
            for f in P.ops["vector"]:
                f(e)

        @blk.scalar
        def _(e):
            for f in P.ops["scalar"]:
                f(e)

    ctx.close()
    return nc


def kernel(L0_attn_hidden_states, L0_attn_W_in, L0_attn_conv_w, L0_attn_W_x,
           L0_attn_W_dt, L0_attn_b_dt, L0_attn_D, L0_attn_W_out, L0_ffn_weight):
    x = np.ascontiguousarray(np.asarray(L0_attn_hidden_states, dtype=np.float32).reshape(NCORES * TOK, H))
    w1t = np.ascontiguousarray(np.asarray(L0_attn_W_in, dtype=np.float32).T)
    w2t = np.ascontiguousarray(np.asarray(L0_attn_conv_w, dtype=np.float32).T)
    w3t = np.ascontiguousarray(np.asarray(L0_attn_W_out, dtype=np.float32).T)
    w4t = np.ascontiguousarray(np.asarray(L0_ffn_weight, dtype=np.float32).T)
    dcol = np.ascontiguousarray(np.asarray(L0_attn_D, dtype=np.float32).reshape(IB, 128).T)
    id16 = np.eye(128, dtype=ml_dtypes.bfloat16)
    id32 = np.eye(128, dtype=np.float32)

    nc = build_core_program()
    in_maps = []
    for c in range(NCORES):
        in_maps.append({
            "x0": x[c * TOK:(c + 1) * TOK],
            "w1t": w1t, "w2t": w2t, "w3t": w3t, "w4t": w4t,
            "dcol": dcol, "id16": id16, "id32": id32,
        })
    res = run_bass_kernel_spmd(nc, in_maps, core_ids=list(range(NCORES)))
    out = np.concatenate([np.asarray(r["out"]) for r in res.results], axis=0)
    return out.reshape(4, 2048, H).astype(np.float32)


# version 34
# speedup vs baseline: 1.1145x; 1.0873x over previous
import numpy as np
import ml_dtypes
from contextlib import ExitStack

import concourse.bass as bass
import concourse.mybir as mybir
from concourse.bass_utils import run_bass_kernel_spmd

F32 = mybir.dt.float32
BF16 = mybir.dt.bfloat16
FP8 = mybir.dt.float8e4
AF = mybir.ActivationFunctionType
ALU = mybir.AluOpType
AX = mybir.AxisListType

NCORES = 8
TOK = 1024
H = 2048
I = 4096
TT = TOK // 128
TH = TOK // 512
HB = H // 128
IB = I // 128
INV224 = float(np.float32(1.0) / np.float32(224.0))


class Plan:
    ENGINES = ("sync", "gpsimd", "tensor", "vector", "scalar")

    def __init__(self):
        self.ops = {e: [] for e in self.ENGINES}
        self.cnt = {}
        self.lastwait = {}

    def c(self, sem):
        return self.cnt.get(sem, 0)

    def inc(self, sem, n=1):
        self.cnt[sem] = self.c(sem) + n
        return self.cnt[sem]

    def emit(self, eng, fn):
        self.ops[eng].append(fn)

    def wait(self, eng, sem_name, val, sems):
        if val <= 0:
            return
        key = (eng, sem_name)
        if self.lastwait.get(key, -1) >= val:
            return
        self.lastwait[key] = val
        sem = sems[sem_name]
        self.emit(eng, lambda e, s=sem, v=val: e.wait_ge(s, v))


def build_core_program():
    nc = bass.Bass("TRN2", target_bir_lowering=False)

    x0_d = nc.dram_tensor("x0", [TOK, H], F32, kind="ExternalInput").ap()
    w1_d = nc.dram_tensor("w1t", [H, 2 * I], F32, kind="ExternalInput").ap()
    w2_d = nc.dram_tensor("w2t", [I, I], F32, kind="ExternalInput").ap()
    w3_d = nc.dram_tensor("w3t", [I, H], F32, kind="ExternalInput").ap()
    w4_d = nc.dram_tensor("w4t", [H, H], F32, kind="ExternalInput").ap()
    dcol_d = nc.dram_tensor("dcol", [128, IB], F32, kind="ExternalInput").ap()
    id16_d = nc.dram_tensor("id16", [128, 128], BF16, kind="ExternalInput").ap()
    id32_d = nc.dram_tensor("id32", [128, 128], F32, kind="ExternalInput").ap()
    x1s_d = nc.dram_tensor("x1s", [TOK, H], F32).ap()
    out_d = nc.dram_tensor("out", [TOK, H], F32, kind="ExternalOutput").ap()

    ctx = ExitStack()
    sb = lambda name, cols, dt: ctx.enter_context(nc.sbuf_tensor(name, [128, cols], dt))

    A = sb("arena_a", HB * TOK, BF16)
    B = sb("arena_b", 2 * HB * TOK, BF16)
    C = sb("arena_c", IB * TOK, BF16)
    W1 = [sb("w1a", HB * 128, BF16), sb("w1b", HB * 128, BF16)]
    W2 = [sb("w2a", IB * 128, BF16), sb("w2b", IB * 128, BF16)]
    S = [sb("s_a", H, F32), sb("s_b", H, F32)]
    q8 = sb("q8", H, FP8)
    q8w = sb("q8w", 512, FP8)
    id16 = sb("id16s", 128, BF16)
    id32 = sb("id32s", 128, F32)
    dcols = sb("dcols", IB, F32)
    ones = sb("ones", 128, F32)
    eps = sb("eps", 1, F32)
    bns8a = sb("bns8a", 4 * 24, F32)
    mv8a = sb("mv8a", TT * 2, F32)
    std8a = sb("std8a", TT, F32)
    rstd8a = sb("rstd8a", TT, F32)
    mr8a = sb("mr8a", TT, F32)
    mv8b = sb("mv8b", TT * 2, F32)
    std8b = sb("std8b", TT, F32)
    rstd8b = sb("rstd8b", TT, F32)
    am16P = [sb("am160", 16, F32), sb("am161", 16, F32)]
    qsAP = [sb("qsA0", 16, F32), sb("qsA1", 16, F32)]
    dqAP = [sb("dqA0", 16, F32), sb("dqA1", 16, F32)]
    wS = sb("wS", 64, F32)
    wam = sb("wam", 1, F32)
    wqs_c = sb("wqs_c", 1, F32)
    wdq_c = sb("wdq_c", 1, F32)
    wrow = sb("wrow", 128, F32)
    WQSLOT = {2: 0, 3: 1, 0: 2, 1: 0}
    wqs_b = sb("wqs_b", 3 * 64, F32)
    wdq_b = sb("wdq_b", 3 * 64, F32)
    acc1 = sb("acc1", TT, F32)
    ssq1 = sb("ssq1", TT, F32)
    junk2 = sb("junk2", 8, F32)
    accH = sb("accH", TT * 4, F32)

    ln0T = A
    attn_tm = A
    xT = B
    zs = C
    ffn_tm = C
    W4slab = [A[:, 0:8192], A[:, 8192:16384],
              None,
              C[:, 24576:32768]]

    W4halves = {0: (W4slab[0],), 1: (W4slab[1],),
                2: (W2[0].ap() if hasattr(W2[0], "ap") else W2[0], W2[1]),
                3: (W4slab[3],)}
    W4halves[2] = (W2[0], W2[1])
    ps = [ctx.enter_context(nc.psum_tensor(f"ps{i}", [128, 512], F32)) for i in range(5)]
    psT = [ctx.enter_context(nc.psum_tensor("psTa", [128, 1024], BF16)),
           ctx.enter_context(nc.psum_tensor("psTb", [128, 1024], BF16))]
    psF = ctx.enter_context(nc.psum_tensor("psF", [128, 512], F32))

    semnames = ["qA1", "qA2", "qB1", "qB2", "qSa", "qSb", "qSa2", "qSb2", "qOa", "qOb",
                "qOc", "qOd", "qM",
                "sT", "sV", "sS", "sG",
                "qW40", "qW41", "qW42", "qW43", "qW3p", "qW3q"]
    sems = {n: ctx.enter_context(nc.semaphore(n)) for n in semnames}
    qsbuf = ["qSa", "qSb"]
    qsbuf2 = ["qSa2", "qSb2"]

    P = Plan()

    def op(eng, name, *args, inc=None, incn=1, **kw):
        if inc is not None:
            v = P.inc(inc, incn)
            sem = sems[inc]
            P.emit(eng, lambda e, a=args, k=kw, s=sem, n=incn: getattr(e, name)(*a, **k).then_inc(s, n))
            return v
        P.emit(eng, lambda e, a=args, k=kw: getattr(e, name)(*a, **k))
        return None

    def drain_inc(eng, sem_name, n=1):
        v = P.inc(sem_name, n)
        sem = sems[sem_name]
        P.emit(eng, lambda e, s=sem, n_=n: e.drain().then_inc(s, n_))
        return v

    def drain(eng):
        P.emit(eng, lambda e: e.drain())

    bank_state = [None] * 5
    bank_rr = [0]

    def acquire_bank(eng):
        b = bank_rr[0] % 5
        bank_rr[0] += 1
        if bank_state[b] is not None:
            sname, val = bank_state[b]
            P.wait(eng, sname, val, sems)
        return b

    psT_state = [None] * 2
    psT_grp = [0]

    def transpose_tile_group(src_stage, dst_tensor, tt, evac_eng="scalar"):
        v = None
        for g in range(4):
            bank = psT_grp[0] % 2
            psT_grp[0] += 1
            if psT_state[bank] is not None:
                sname, val = psT_state[bank]
                P.wait("tensor", sname, val, sems)
            for j in range(4):
                hb = g * 4 + j
                tr = op("tensor", "transpose", psT[bank][:, j * 128:(j + 1) * 128],
                        src_stage[:, hb * 128:(hb + 1) * 128], id16[:, :],
                        inc=("sT" if j == 3 else None))
            P.wait(evac_eng, "sT", tr, sems)
            sem = "sV" if evac_eng == "vector" else "sS"
            out3 = bass.AP(dst_tensor, (g * 4) * TOK + tt * 128,
                           [[dst_tensor.shape[1], 128], [TOK, 4], [1, 128]])
            v = op(evac_eng, "tensor_copy" if evac_eng == "vector" else "copy",
                   out3, psT[bank][:, 0:512].rearrange("p (b j) -> p b j", j=128), inc=sem)
            psT_state[bank] = (sem, v)
        return v, tr

    def colbcast(t, ncol_blocks, offset=0, blkstride=1):
        pstride = t.shape[1]
        return bass.AP(t, offset, [[pstride, 128], [blkstride, ncol_blocks], [0, 128]])

    op("sync", "dma_start", id16[:, :], id16_d[:, :], inc="qM", incn=16)
    op("sync", "dma_start", id32[:, :], id32_d[:, :], inc="qM", incn=16)
    op("sync", "dma_start", dcols[:, :], dcol_d[:, :], inc="qM", incn=16)
    qm_total = P.c("qM")
    op("gpsimd", "memset", ones[:, :], 1.0)
    op("gpsimd", "memset", eps[:, :], 1e-5)
    sg_ready = drain_inc("gpsimd", "sG")

    def stats_to(setname, tt, src_ap_fn):
        assert setname == "a"
        bns, mv, std = bns8a, mv8a, std8a
        sl = (tt % 4) * 24
        for i in range(4):
            op("vector", "bn_stats", bns[:, sl + i * 6: sl + (i + 1) * 6],
               src_ap_fn(i * 512, (i + 1) * 512))
        drain("vector")
        m = op("vector", "bn_aggr", mv[:, tt * 2:(tt + 1) * 2],
               bns[:, sl:sl + 24], inc="sV")
        P.wait("scalar", "sV", m, sems)
        P.wait("scalar", "sG", sg_ready, sems)
        sq = op("scalar", "activation", std[:, tt:tt + 1], mv[:, tt * 2 + 1: tt * 2 + 2],
                AF.Sqrt, inc="sS", bias=eps[:, 0:1])
        return m, sq

    def finish_rstd(setname, tt, sq, with_mr=False):
        mv, rstd = (mv8a, rstd8a) if setname == "a" else (mv8b, rstd8b)
        P.wait("vector", "sS", sq, sems)
        op("vector", "reciprocal", rstd[:, tt:tt + 1],
           (std8a if setname == "a" else std8b)[:, tt:tt + 1])
        drain("vector")
        if with_mr:
            return op("vector", "tensor_scalar", mr8a[:, tt:tt + 1], mv[:, tt * 2:tt * 2 + 1],
                      rstd[:, tt:tt + 1], -1.0, ALU.mult, ALU.mult, inc="sV")
        return drain_inc("vector", "sV")

    s_release = [0, 0]
    s_loads = [0, 0]
    ln0_evac_v = 0
    ln0_evac_s = 0

    s_loads2 = [0, 0]

    def xload(tt, src_d):
        p = tt % 2
        P.wait("sync", "sV", s_release[p], sems)
        s_loads[p] += 1
        op("sync", "dma_start", S[p][:, 0:1024], src_d[tt * 128:(tt + 1) * 128, 0:1024],
           inc=qsbuf[p], incn=16)
        P.wait("gpsimd", "sV", s_release[p], sems)
        s_loads2[p] += 1
        op("gpsimd", "dma_start", S[p][:, 1024:2048], src_d[tt * 128:(tt + 1) * 128, 1024:2048],
           inc=qsbuf2[p], incn=16)
        return 16 * s_loads[p], 16 * s_loads2[p]

    b_sq = {}
    b_need = {}
    b_need[0] = xload(0, x0_d)

    def b_stats(tt):
        p = tt % 2
        n1, n2 = b_need[tt]
        bns, mv, std = bns8a, mv8a, std8a
        sl = (tt % 4) * 24
        P.wait("vector", qsbuf[p], n1, sems)
        for i in range(2):
            op("vector", "bn_stats", bns[:, sl + i * 6: sl + (i + 1) * 6],
               S[p][:, i * 512:(i + 1) * 512])
        P.wait("vector", qsbuf2[p], n2, sems)
        for i in range(2, 4):
            op("vector", "bn_stats", bns[:, sl + i * 6: sl + (i + 1) * 6],
               S[p][:, i * 512:(i + 1) * 512])
        drain("vector")
        m = op("vector", "bn_aggr", mv[:, tt * 2:(tt + 1) * 2],
               bns[:, sl:sl + 24], inc="sV")
        P.wait("scalar", "sV", m, sems)
        P.wait("scalar", "sG", sg_ready, sems)
        b_sq[tt] = op("scalar", "activation", std[:, tt:tt + 1], mv[:, tt * 2 + 1: tt * 2 + 2],
                      AF.Sqrt, inc="sS", bias=eps[:, 0:1])

    def b_apply(tt):
        p = tt % 2
        finish_rstd("a", tt, b_sq[tt])
        stg = B[:, p * H:(p + 1) * H]
        ap_done = op("vector", "tensor_scalar", stg, S[p][:, :],
                     mv8a[:, tt * 2:tt * 2 + 1], rstd8a[:, tt:tt + 1],
                     ALU.subtract, ALU.mult, inc="sV")
        s_release[p] = ap_done
        P.wait("tensor", "qM", qm_total, sems)
        P.wait("tensor", "sV", ap_done, sems)
        v, _ = transpose_tile_group(stg, ln0T, tt, evac_eng="scalar")
        return {"scalar": v}

    b_stats(0)
    for tt in range(TT):
        if tt + 1 < TT:
            b_need[tt + 1] = xload(tt + 1, x0_d)
            b_stats(tt + 1)
        vals = b_apply(tt)
        ln0_evac_v = vals.get("vector", ln0_evac_v)
        ln0_evac_s = vals.get("scalar", ln0_evac_s)

    w_consumed = {}
    half_count = {n: 0 for n in ("qA1", "qA2", "qB1", "qB2")}
    last_on_parity = [None, None]

    def weight_dma2(stream, idx, buf, src_cols_ap, nblk, blkcols):
        pbuf = idx % 2
        key = last_on_parity[pbuf]
        if key is not None and key in w_consumed:
            P.wait("gpsimd", "sT", w_consumed[key], sems)
        last_on_parity[pbuf] = (stream, idx)
        h = nblk // 2
        sem1 = ("qA1", "qB1")[pbuf]
        sem2 = ("qA2", "qB2")[pbuf]
        half_count[sem1] += 1
        op("gpsimd", "dma_start",
           buf[:, 0:h * blkcols].rearrange("p (b c) -> p b c", c=blkcols),
           src_cols_ap[0:h * 128, :].rearrange("(b p) c -> p b c", p=128),
           inc=sem1, incn=16)
        half_count[sem2] += 1
        op("gpsimd", "dma_start",
           buf[:, h * blkcols:].rearrange("p (b c) -> p b c", c=blkcols),
           src_cols_ap[h * 128:, :].rearrange("(b p) c -> p b c", p=128),
           inc=sem2, incn=16)
        return sem1, 16 * half_count[sem1], sem2, 16 * half_count[sem2]

    P.wait("tensor", "sV", ln0_evac_v, sems)
    P.wait("tensor", "sS", ln0_evac_s, sems)
    xT_all = 0
    zs_all = 0
    for og in range(2 * IB):
        pbuf = og % 2
        s1, v1, s2, v2 = weight_dma2("w1", og, W1[pbuf],
                                     w1_d[:, og * 128:(og + 1) * 128], HB, 128)
        P.wait("tensor", s1, v1, sems)
        for th in range(TH):
            b = acquire_bank("tensor")
            for hb in range(HB):
                if hb == HB // 2:
                    P.wait("tensor", s2, v2, sems)
                last = hb == HB - 1
                g = op("tensor", "matmul", ps[b][:, :],
                       W1[pbuf][:, hb * 128:(hb + 1) * 128],
                       ln0T[:, hb * TOK + th * 512: hb * TOK + (th + 1) * 512],
                       start=(hb == 0), stop=last, inc=("sT" if last else None))
            if og < IB:
                P.wait("vector", "sT", g, sems)
                v = op("vector", "tensor_copy",
                       xT[:, og * TOK + th * 512: og * TOK + (th + 1) * 512],
                       ps[b][:, :], inc="sV")
                bank_state[b] = ("sV", v)
                xT_all = v
            else:
                P.wait("scalar", "sT", g, sems)
                v = op("scalar", "activation",
                       zs[:, (og - IB) * TOK + th * 512: (og - IB) * TOK + (th + 1) * 512],
                       ps[b][:, :], AF.Silu, inc="sS")
                bank_state[b] = ("sS", v)
                zs_all = v
        w_consumed[("w1", og)] = g

    P.wait("tensor", "sV", xT_all, sems)
    mm2_last = None
    yz_all = None
    for og in range(IB):
        pbuf = og % 2
        s1, v1, s2, v2 = weight_dma2("w2", og, W2[pbuf],
                                     w2_d[:, og * 128:(og + 1) * 128], IB, 128)
        P.wait("tensor", s1, v1, sems)
        for th in range(TH):
            b = acquire_bank("tensor")
            for kb in range(IB):
                if kb == IB // 2:
                    P.wait("tensor", s2, v2, sems)
                last = kb == IB - 1
                g = op("tensor", "matmul", ps[b][:, :],
                       W2[pbuf][:, kb * 128:(kb + 1) * 128],
                       xT[:, kb * TOK + th * 512: kb * TOK + (th + 1) * 512],
                       start=(kb == 0), stop=last, inc=("sT" if last else None))
            mm2_last = g
            P.wait("vector", "sT", g, sems)
            P.wait("vector", "sS", zs_all, sems)
            P.wait("vector", "qM", qm_total, sems)
            cols = slice(og * TOK + th * 512, og * TOK + (th + 1) * 512)
            v = op("vector", "scalar_tensor_tensor", zs[:, cols], ps[b][:, :],
                   dcols[:, og:og + 1], zs[:, cols], ALU.mult, ALU.mult, inc="sV")
            bank_state[b] = ("sV", v)
            yz_all = v
        w_consumed[("w2", og)] = mm2_last

    w4sem = ["qW40", "qW41", "qW42", "qW43"]
    f1_mr = {}
    x1done = {}
    ln2done = {}
    ln2q_free = [0]
    ln2stageP = [B[:, HB * TOK: HB * TOK + H],
                 B[:, HB * TOK + 3 * H: HB * TOK + 4 * H]]
    ln2qstage = B[:, HB * TOK + H: HB * TOK + 2 * H]
    junk = B[:, HB * TOK + 2 * H: HB * TOK + 3 * H]
    qo_cnt = [0, 0]
    qosem = ["qOa", "qOb"]
    x1_stores_p = [0, 0]
    q2_evac_s = 0
    wq_done = {}
    wq_scales_done = {}
    wq_prev_reads = [0]
    q8_free = [0]
    f2s_done = {}

    def w4_dma(og, gate, gate2=None):
        halves = W4halves[og]
        if len(halves) == 1:
            P.wait("gpsimd", gate[0], gate[1], sems)
            op("gpsimd", "dma_start",
               halves[0].rearrange("p (b c) -> p b c", c=512),
               w4_d[:, og * 512:(og + 1) * 512].rearrange("(b p) c -> p b c", p=128),
               inc=w4sem[og], incn=16)
            return 16
        P.wait("gpsimd", gate[0], gate[1], sems)
        op("gpsimd", "dma_start",
           halves[0].rearrange("p (b c) -> p b c", c=512),
           w4_d[0:1024, og * 512:(og + 1) * 512].rearrange("(b p) c -> p b c", p=128),
           inc=w4sem[og], incn=16)
        if gate2 is not None:
            P.wait("gpsimd", gate2[0], gate2[1], sems)
        op("gpsimd", "dma_start",
           halves[1].rearrange("p (b c) -> p b c", c=512),
           w4_d[1024:2048, og * 512:(og + 1) * 512].rearrange("(b p) c -> p b c", p=128),
           inc=w4sem[og], incn=16)
        return 32


    def wq_scales(og, dma_need=16):
        P.wait("vector", w4sem[og], dma_need, sems)
        P.wait("vector", "sT", wq_prev_reads[0], sems)
        halves = W4halves[og]
        if len(halves) == 1:
            op("vector", "tensor_reduce", wS[:, :],
               halves[0].rearrange("p (b c) -> p b c", c=128), AX.X, ALU.max,
               apply_absolute_value=True)
        else:
            op("vector", "tensor_reduce", wS[:, 0:32],
               halves[0].rearrange("p (b c) -> p b c", c=128), AX.X, ALU.max,
               apply_absolute_value=True)
            op("vector", "tensor_reduce", wS[:, 32:64],
               halves[1].rearrange("p (b c) -> p b c", c=128), AX.X, ALU.max,
               apply_absolute_value=True)
        v1 = drain_inc("vector", "sV")
        P.wait("tensor", "sV", v1, sems)
        tr1 = op("tensor", "transpose", psF[0:64, 0:128], wS[:, 0:64], id32[:, :], inc="sT")
        P.wait("vector", "sT", tr1, sems)
        op("vector", "tensor_reduce", wam[0:64, 0:1], psF[0:64, 0:128], AX.X, ALU.max)
        drain("vector")
        op("vector", "tensor_scalar_max", wam[0:64, 0:1], wam[0:64, 0:1], 1e-4)
        drain("vector")
        op("vector", "reciprocal", wqs_c[0:64, 0:1], wam[0:64, 0:1])
        drain("vector")
        op("vector", "tensor_scalar", wqs_c[0:64, 0:1], wqs_c[0:64, 0:1], 224.0, None, ALU.mult)
        op("vector", "tensor_scalar", wdq_c[0:64, 0:1], wam[0:64, 0:1], INV224, None, ALU.mult)
        drain("vector")
        op("vector", "tensor_scalar", wrow[0:64, 0:128], ones[0:64, 0:128],
           wqs_c[0:64, 0:1], None, ALU.mult)
        v2 = drain_inc("vector", "sV")
        P.wait("tensor", "sV", v2, sems)
        tr2 = op("tensor", "transpose", psF[:, 128:192], wrow[0:64, 0:128],
                 id32[0:64, 0:64], inc="sT")
        P.wait("vector", "sT", tr2, sems)
        op("vector", "tensor_copy", wqs_b[:, WQSLOT[og] * 64:(WQSLOT[og] + 1) * 64], psF[:, 128:192])
        drain("vector")
        op("vector", "tensor_scalar", wrow[0:64, 0:128], ones[0:64, 0:128],
           wdq_c[0:64, 0:1], None, ALU.mult)
        v3 = drain_inc("vector", "sV")
        P.wait("tensor", "sV", v3, sems)
        tr3 = op("tensor", "transpose", psF[:, 192:256], wrow[0:64, 0:128],
                 id32[0:64, 0:64], inc="sT")
        wq_prev_reads[0] = tr3
        P.wait("vector", "sT", tr3, sems)
        op("vector", "tensor_copy", wdq_b[:, WQSLOT[og] * 64:(WQSLOT[og] + 1) * 64], psF[:, 192:256])
        wq_scales_done[og] = drain_inc("vector", "sV")

    def wq_rt_chunk(og, ch):
        halves = W4halves[og]
        if len(halves) == 1:
            part = halves[0][:, ch * 2048:(ch + 1) * 2048]
        else:
            part = halves[ch // 2][:, (ch % 2) * 2048:((ch % 2) + 1) * 2048]
        op("vector", "tensor_tensor",
           q8.rearrange("p (b j) -> p b j", j=128),
           part.rearrange("p (b j) -> p b j", j=128),
           colbcast(wqs_b, 16, offset=WQSLOT[og] * 64 + ch * 16), ALU.mult)
        drain("vector")
        d = op("vector", "tensor_tensor",
               part.rearrange("p (b j) -> p b j", j=128),
               q8.rearrange("p (b j) -> p b j", j=128),
               colbcast(wdq_b, 16, offset=WQSLOT[og] * 64 + ch * 16), ALU.mult,
               inc=("sV" if ch == 3 else None))
        drain("vector")
        if ch == 3:
            wq_done[og] = [("sV", d)]
            q8_free[0] = d

    def _wq_part(og, ch):
        halves = W4halves[og]
        if len(halves) == 1:
            return halves[0][:, ch * 2048:(ch + 1) * 2048]
        return halves[ch // 2][:, (ch % 2) * 2048:((ch % 2) + 1) * 2048]

    def wq_rt_v_sub(og, ch, sub, mark=False):
        pp = _wq_part(og, ch)[:, sub * 512:(sub + 1) * 512]
        off = WQSLOT[og] * 64 + ch * 16 + sub * 4
        P.wait("vector", "sV", wq_scales_done[og], sems)
        op("vector", "tensor_tensor",
           q8w.rearrange("p (b j) -> p b j", j=128),
           pp.rearrange("p (b j) -> p b j", j=128),
           colbcast(wqs_b, 4, offset=off), ALU.mult)
        drain("vector")
        d = op("vector", "tensor_tensor",
               pp.rearrange("p (b j) -> p b j", j=128),
               q8w.rearrange("p (b j) -> p b j", j=128),
               colbcast(wdq_b, 4, offset=off), ALU.mult,
               inc=("sV" if mark else None))
        drain("vector")
        return d

    def wq_rt_vc(og, ch, mark=False):
        d = None
        for sub in range(4):
            d = wq_rt_v_sub(og, ch, sub, mark=(mark and sub == 3))
        return d

    sq2 = {}
    subd_m = {}
    def f_load(tt):
        p = tt % 2
        P.wait("sync", "sV", s_release[p], sems)
        P.wait("sync", qosem[p], 16 * x1_stores_p[p], sems)
        s_loads[p] += 1
        op("sync", "dma_start", S[p][:, :], x0_d[tt * 128:(tt + 1) * 128, :],
           inc=qsbuf[p], incn=16)

    def f_subd(tt):
        p = tt % 2
        P.wait("scalar", "sV", f1_mr[tt], sems)
        P.wait("scalar", qsbuf[p], 16 * s_loads[p], sems)
        P.wait("scalar", qsbuf2[p], 16 * s_loads2[p], sems)
        subd_m[tt] = op("scalar", "add", S[p][:, :], S[p][:, :], mr8a[:, tt:tt + 1],
                        inc="sS")

    def f_stt(tt):
        p = tt % 2
        P.wait("vector", "sS", subd_m[tt], sems)
        x1done[tt] = op("vector", "scalar_tensor_tensor", S[p][:, :],
                        attn_tm[:, tt * H:(tt + 1) * H], rstd8a[:, tt:tt + 1], S[p][:, :],
                        ALU.mult, ALU.add, accum_out=acc1[:, tt:tt + 1], inc="sV")
        drain("vector")

    def f_st1b(tt):
        p = tt % 2
        P.wait("scalar", "sV", x1done[tt], sems)
        x1_stores_p[p] += 1
        op("scalar", "dma_start", x1s_d[tt * 128:(tt + 1) * 128, :], S[p][:, :],
           inc=qosem[p], incn=16)
        op("scalar", "activation", junk[:, :], S[p][:, :], AF.Square,
           accum_out=ssq1[:, tt:tt + 1])
        op("scalar", "mul", mv8b[:, tt * 2:tt * 2 + 1], acc1[:, tt:tt + 1], 1.0 / H)
        drain("scalar")
        op("scalar", "mul", junk2[:, 0:1], mv8b[:, tt * 2:tt * 2 + 1],
           mv8b[:, tt * 2:tt * 2 + 1])
        op("scalar", "mul", junk2[:, 2:3], ssq1[:, tt:tt + 1], 1.0 / H)
        drain("scalar")
        op("scalar", "mul", junk2[:, 1:2], junk2[:, 0:1], -1.0)
        drain("scalar")
        op("scalar", "add", mv8b[:, tt * 2 + 1:tt * 2 + 2], junk2[:, 2:3], junk2[:, 1:2])
        drain("scalar")
        sq2[tt] = op("scalar", "activation", std8b[:, tt:tt + 1],
                     mv8b[:, tt * 2 + 1:tt * 2 + 2], AF.Sqrt, inc="sS", bias=eps[:, 0:1])

    def f_st2a(tt):
        p = tt % 2
        stg = ln2stageP[p]
        P.wait("vector", "sS", sq2[tt], sems)
        op("vector", "reciprocal", rstd8b[:, tt:tt + 1], std8b[:, tt:tt + 1])
        drain("vector")
        lnd = op("vector", "tensor_scalar", stg[:, :], S[p][:, :],
                 mv8b[:, tt * 2:tt * 2 + 1], rstd8b[:, tt:tt + 1],
                 ALU.subtract, ALU.mult, inc="sV")
        drain("vector")
        ln2done[tt] = lnd
        s_release[p] = lnd
        op("vector", "tensor_reduce", am16P[p][:, :],
           stg.rearrange("p (b c) -> p b c", c=128), AX.X, ALU.max,
           apply_absolute_value=True)
        drain("vector")
        op("vector", "tensor_scalar_max", am16P[p][:, :], am16P[p][:, :], 1e-4)
        drain("vector")
        op("vector", "reciprocal", qsAP[p][:, :], am16P[p][:, :])
        drain("vector")
        op("vector", "tensor_scalar", qsAP[p][:, :], qsAP[p][:, :], 224.0, None, ALU.mult)
        op("vector", "tensor_scalar", dqAP[p][:, :], am16P[p][:, :], INV224, None, ALU.mult)
        drain("vector")

    q_done = {}

    def f_q(tt):
        p = tt % 2
        op("vector", "tensor_tensor",
           q8.rearrange("p (b j) -> p b j", j=128),
           ln2stageP[p].rearrange("p (b j) -> p b j", j=128),
           colbcast(qsAP[p], HB), ALU.mult)
        drain("vector")
        P.wait("vector", "sT", ln2q_free[0], sems)
        d = op("vector", "tensor_tensor",
               ln2qstage.rearrange("p (b j) -> p b j", j=128),
               q8.rearrange("p (b j) -> p b j", j=128),
               colbcast(dqAP[p], HB), ALU.mult, inc="sV")
        drain("vector")
        q_done[tt] = d

    def f_tr(tt):
        nonlocal q2_evac_s
        P.wait("tensor", "sV", q_done[tt], sems)
        v, tr = transpose_tile_group(ln2qstage, B, tt, evac_eng="scalar")
        ln2q_free[0] = tr
        q2_evac_s = v

    def mm4_group(og, tt):
        halves = W4halves[og]
        b = acquire_bank("tensor")
        for kb in range(HB):
            if len(halves) == 1:
                rhs = halves[0][:, kb * 512:(kb + 1) * 512]
            else:
                rhs = halves[kb // 8][:, (kb % 8) * 512:((kb % 8) + 1) * 512]
            last = kb == HB - 1
            g = op("tensor", "matmul", ps[b][:, :],
                   B[:, kb * TOK + tt * 128: kb * TOK + (tt + 1) * 128],
                   rhs,
                   start=(kb == 0), stop=last, inc=("sT" if last else None))
        P.wait("scalar", "sT", g, sems)
        v = op("scalar", "activation",
               ffn_tm[:, tt * H + og * 512: tt * H + (og + 1) * 512],
               ps[b][:, :], AF.Copy, accum_out=accH[:, tt * 4 + og: tt * 4 + og + 1],
               inc="sS")
        bank_state[b] = ("sS", v)
        return v

    P.wait("tensor", "sV", yz_all, sems)
    f_load(0)
    f_load(1)
    P.wait("gpsimd", "sT", w_consumed[("w1", 2 * IB - 1)], sems)
    op("gpsimd", "dma_start", W1[0][:, :].rearrange("p (b c) -> p b c", c=512),
       w3_d[0:512, 0:512].rearrange("(b p) c -> p b c", p=128), inc="qW3p", incn=16)
    op("gpsimd", "dma_start", W1[1][:, :].rearrange("p (b c) -> p b c", c=512),
       w3_d[512:1024, 0:512].rearrange("(b p) c -> p b c", p=128), inc="qW3q", incn=16)
    attn_done = {}
    mm3_last = None
    f1_sq = {}
    for hs in range(4):
        pbuf = hs % 2
        slab = B[:, pbuf * (HB * TOK):(pbuf * (HB * TOK)) + IB * 512]
        if hs < 2:
            P.wait("gpsimd", "sT", mm2_last, sems)
        if hs == 0:
            qsems = ["qA2", "qB1", "qB2"]
            qneed = [None]
            for qi, q in enumerate((1, 2, 3)):
                half_count[qsems[qi]] += 1
                op("gpsimd", "dma_start",
                   slab[:, q * 4096:(q + 1) * 4096].rearrange("p (b c) -> p b c", c=512),
                   w3_d[q * 1024:(q + 1) * 1024, 0:512].rearrange("(b p) c -> p b c", p=128),
                   inc=qsems[qi], incn=16)
                qneed.append((qsems[qi], 16 * half_count[qsems[qi]]))
            last_on_parity[0] = ("w3q", 0)
            last_on_parity[1] = ("w3q", 0)
        else:
            s1, v1, s2, v2 = weight_dma2("w3", hs, slab,
                                         w3_d[:, hs * 512:(hs + 1) * 512], IB, 512)
        if hs == 1:
            w4_dma(2, ("sT", w_consumed[("w2", IB - 2)]),
                   gate2=("sT", w_consumed[("w2", IB - 1)]))
        if hs == 0:
            P.wait("tensor", "qW3p", 16, sems)
            P.wait("tensor", "qW3q", 16, sems)
        else:
            P.wait("tensor", s1, v1, sems)
        for tt in range(TT):
            b = acquire_bank("tensor")
            for kb in range(IB):
                if hs == 0:
                    if kb in (8, 16, 24):
                        P.wait("tensor", qneed[kb // 8][0], qneed[kb // 8][1], sems)
                elif kb == IB // 2:
                    P.wait("tensor", s2, v2, sems)
                last = kb == IB - 1
                if hs == 0 and kb < 8:
                    rhs = W1[kb // 4][:, (kb % 4) * 512:((kb % 4) + 1) * 512]
                else:
                    rhs = slab[:, kb * 512:(kb + 1) * 512]
                g = op("tensor", "matmul", ps[b][:, :],
                       zs[:, kb * TOK + tt * 128: kb * TOK + (tt + 1) * 128],
                       rhs,
                       start=(kb == 0), stop=last, inc=("sT" if last else None))
            P.wait("scalar", "sT", g, sems)
            v = op("scalar", "copy",
                   attn_tm[:, tt * H + hs * 512: tt * H + (hs + 1) * 512],
                   ps[b][:, :], inc="sS")
            bank_state[b] = ("sS", v)
            if hs == 0 and tt == 0:
                w_consumed[("w3q", 0)] = g
            if hs == 3:
                attn_done[tt] = v
                P.wait("vector", "sS", v, sems)
                _, sq = stats_to("a", tt, lambda a, b_, tt=tt: attn_tm[:, tt * H + a: tt * H + b_])
                f1_sq[tt] = sq
                f1_mr[tt] = finish_rstd("a", tt, sq, with_mr=True)
                if tt == 4:
                    f_subd(0)
                    f_stt(0)
                if tt == 5:
                    f_subd(1)
                    f_stt(1)
        w_consumed[("w3", hs)] = g
        mm3_last = g
        if hs == 0:
            last_on_parity[0] = ("w3", 0)
        if hs == 2:
            wq_scales(2, dma_need=32)
            for ch in range(4):
                wq_rt_chunk(2, ch)




    w4_dma(3, ("sT", mm3_last))
    wq3_last = None
    RT3 = {3: (0, 3), 4: (3, 6), 5: (6, 9), 6: (9, 12), 7: (12, 16)}
    for t in range(TT):
        if 1 <= t and t + 1 < TT:
            f_stt(t + 1)
        f_st1b(t)
        if t >= 1:
            f_tr(t - 1)
        f_st2a(t)
        f_q(t)
        if t >= 1:
            if t == 1:
                for s_, v_ in wq_done[2]:
                    P.wait("tensor", s_, v_, sems)
            P.wait("tensor", "sS", q2_evac_s, sems)
            mm4_group(2, t - 1)
        if t + 2 < TT:
            f_load(t + 2)
            f_subd(t + 2)
        if t == 2:
            wq_scales(3)
        if t == 3:
            w4_dma(0, ("sV", x1done[3]))
        if t >= 3:
            for gidx in range(*RT3[t]):
                wq3_last = wq_rt_v_sub(3, gidx // 4, gidx % 4, mark=(gidx == 15))
    f_tr(TT - 1)
    P.wait("tensor", "sS", q2_evac_s, sems)
    mm4_group(2, TT - 1)
    wq_done[3] = [("sV", wq3_last)]
    w4_dma(1, ("sV", x1done[7]))

    out_stores_p = [0, 0]
    out2_p = [0, 0]
    qosem2 = ["qOc", "qOd"]
    h_srel = [0, 0]
    h_sq = {}
    h_loads = {}

    def h_load(tt):
        p = tt % 2
        rel = s_release[p] if tt < 2 else h_srel[p]
        P.wait("sync", qosem[p], 16 * (x1_stores_p[p] + out_stores_p[p]), sems)
        P.wait("sync", qosem2[p], 16 * out2_p[p], sems)
        P.wait("sync", "sV", rel, sems)
        s_loads[p] += 1
        op("sync", "dma_start", S[p][:, 0:1024], x1s_d[tt * 128:(tt + 1) * 128, 0:1024],
           inc=qsbuf[p], incn=16)
        P.wait("gpsimd", qosem[p], 16 * (x1_stores_p[p] + out_stores_p[p]), sems)
        P.wait("gpsimd", qosem2[p], 16 * out2_p[p], sems)
        P.wait("gpsimd", "sV", rel, sems)
        s_loads2[p] += 1
        op("gpsimd", "dma_start", S[p][:, 1024:2048], x1s_d[tt * 128:(tt + 1) * 128, 1024:2048],
           inc=qsbuf2[p], incn=16)
        h_loads[tt] = (16 * s_loads[p], 16 * s_loads2[p])

    def h_stats(tt):
        drain("scalar")
        op("scalar", "activation", junk[:, :], ffn_tm[:, tt * H:(tt + 1) * H],
           AF.Square, accum_out=ssq1[:, tt:tt + 1])
        drain("scalar")
        op("scalar", "add", junk2[:, 3:4], accH[:, tt * 4:tt * 4 + 1],
           accH[:, tt * 4 + 1:tt * 4 + 2])
        op("scalar", "add", junk2[:, 4:5], accH[:, tt * 4 + 2:tt * 4 + 3],
           accH[:, tt * 4 + 3:tt * 4 + 4])
        op("scalar", "mul", junk2[:, 2:3], ssq1[:, tt:tt + 1], 1.0 / H)
        drain("scalar")
        op("scalar", "add", junk2[:, 3:4], junk2[:, 3:4], junk2[:, 4:5])
        drain("scalar")
        op("scalar", "mul", mv8a[:, tt * 2:tt * 2 + 1], junk2[:, 3:4], 1.0 / H)
        drain("scalar")
        op("scalar", "mul", junk2[:, 0:1], mv8a[:, tt * 2:tt * 2 + 1],
           mv8a[:, tt * 2:tt * 2 + 1])
        drain("scalar")
        op("scalar", "mul", junk2[:, 1:2], junk2[:, 0:1], -1.0)
        drain("scalar")
        op("scalar", "add", mv8a[:, tt * 2 + 1:tt * 2 + 2], junk2[:, 2:3], junk2[:, 1:2])
        drain("scalar")
        h_sq[tt] = op("scalar", "activation", std8a[:, tt:tt + 1],
                      mv8a[:, tt * 2 + 1:tt * 2 + 2], AF.Sqrt, inc="sS", bias=eps[:, 0:1])

    def h_st2(tt):
        p = tt % 2
        P.wait("vector", "sS", h_sq[tt], sems)
        op("vector", "reciprocal", rstd8a[:, tt:tt + 1], std8a[:, tt:tt + 1])
        drain("vector")
        op("vector", "tensor_scalar", mr8a[:, tt:tt + 1], mv8a[:, tt * 2:tt * 2 + 1],
           rstd8a[:, tt:tt + 1], None, ALU.mult)
        drain("vector")
        P.wait("vector", qsbuf[p], h_loads[tt][0], sems)
        P.wait("vector", qsbuf2[p], h_loads[tt][1], sems)
        op("vector", "tensor_scalar", S[p][:, :], S[p][:, :], mr8a[:, tt:tt + 1],
           None, ALU.subtract)
        drain("vector")
        odone = op("vector", "scalar_tensor_tensor", S[p][:, :],
                   ffn_tm[:, tt * H:(tt + 1) * H], rstd8a[:, tt:tt + 1], S[p][:, :],
                   ALU.mult, ALU.add, inc="sV")
        drain("vector")
        h_srel[p] = odone
        P.wait("scalar", "sV", odone, sems)
        out_stores_p[p] += 1
        op("scalar", "dma_start", out_d[tt * 128:(tt + 1) * 128, 0:1024], S[p][:, 0:1024],
           inc=qosem[p], incn=16)
        P.wait("sync", "sV", odone, sems)
        out2_p[p] += 1
        op("sync", "dma_start", out_d[tt * 128:(tt + 1) * 128, 1024:2048], S[p][:, 1024:2048],
           inc=qosem2[p], incn=16)

    h_load(0)
    h_load(1)
    for s_, v_ in wq_done[3]:
        P.wait("tensor", s_, v_, sems)
    for tt in range(TT):
        mm4_group(3, tt)
        if tt == 1:
            wq_scales(1)
    for ch in range(4):
        wq_rt_chunk(1, ch)
    for s_, v_ in wq_done[1]:
        P.wait("tensor", s_, v_, sems)
    for tt in range(TT):
        mm4_group(1, tt)
        if tt == 1:
            wq_scales(0)
    for ch in range(4):
        wq_rt_chunk(0, ch)
    for s_, v_ in wq_done[0]:
        P.wait("tensor", s_, v_, sems)
    mm4_group(0, 0)
    h_stats(0)
    for tt in range(TT):
        if tt + 1 < TT:
            mm4_group(0, tt + 1)
            h_stats(tt + 1)
        h_st2(tt)
        if tt + 2 < TT:
            h_load(tt + 2)

    P.wait("sync", "qOa", 16 * (x1_stores_p[0] + out_stores_p[0]), sems)
    P.wait("sync", "qOb", 16 * (x1_stores_p[1] + out_stores_p[1]), sems)
    P.wait("sync", "qOc", 16 * out2_p[0], sems)
    P.wait("sync", "qOd", 16 * out2_p[1], sems)

    with nc.Block() as blk:
        @blk.sync
        def _(e):
            for f in P.ops["sync"]:
                f(e)

        @blk.gpsimd
        def _(e):
            for f in P.ops["gpsimd"]:
                f(e)

        @blk.tensor
        def _(e):
            for f in P.ops["tensor"]:
                f(e)

        @blk.vector
        def _(e):
            for f in P.ops["vector"]:
                f(e)

        @blk.scalar
        def _(e):
            for f in P.ops["scalar"]:
                f(e)

    ctx.close()
    return nc


def kernel(L0_attn_hidden_states, L0_attn_W_in, L0_attn_conv_w, L0_attn_W_x,
           L0_attn_W_dt, L0_attn_b_dt, L0_attn_D, L0_attn_W_out, L0_ffn_weight):
    x = np.ascontiguousarray(np.asarray(L0_attn_hidden_states, dtype=np.float32).reshape(NCORES * TOK, H))
    w1t = np.ascontiguousarray(np.asarray(L0_attn_W_in, dtype=np.float32).T)
    w2t = np.ascontiguousarray(np.asarray(L0_attn_conv_w, dtype=np.float32).T)
    w3t = np.ascontiguousarray(np.asarray(L0_attn_W_out, dtype=np.float32).T)
    w4t = np.ascontiguousarray(np.asarray(L0_ffn_weight, dtype=np.float32).T)
    dcol = np.ascontiguousarray(np.asarray(L0_attn_D, dtype=np.float32).reshape(IB, 128).T)
    id16 = np.eye(128, dtype=ml_dtypes.bfloat16)
    id32 = np.eye(128, dtype=np.float32)

    nc = build_core_program()
    in_maps = []
    for c in range(NCORES):
        in_maps.append({
            "x0": x[c * TOK:(c + 1) * TOK],
            "w1t": w1t, "w2t": w2t, "w3t": w3t, "w4t": w4t,
            "dcol": dcol, "id16": id16, "id32": id32,
        })
    res = run_bass_kernel_spmd(nc, in_maps, core_ids=list(range(NCORES)))
    out = np.concatenate([np.asarray(r["out"]) for r in res.results], axis=0)
    return out.reshape(4, 2048, H).astype(np.float32)
